# version 08
# baseline (speedup 1.0000x reference)
import math
import numpy as np
import ml_dtypes
import concourse.bass as bass
import concourse.mybir as mybir
from concourse.bass_utils import run_bass_kernel_spmd
from contextlib import ExitStack

F32 = mybir.dt.float32
BF16 = mybir.dt.bfloat16
AF = mybir.ActivationFunctionType
ALU = mybir.AluOpType
AX = mybir.AxisListType.X

S, D, H, KH, M, FF = 2048, 1024, 16, 64, 256, 4096
NU, ND, NFF = S // 128, D // 128, FF // 128
SCALE = 64.0 ** -0.25
SQC = 0.5 * SCALE * SCALE
C0 = 25.0
DPB = math.log(16.0 * 1e-6) - C0
EPHI = 1e-6
ENGS = ("sync", "scalar", "vector", "tensor", "gpsimd")


class Sem:
    __slots__ = ("h", "n")

    def __init__(self, h):
        self.h = h
        self.n = 0


class Prog:
    def __init__(self):
        self.ops = {e: [] for e in ENGS}

    def emit(self, eng, build, waits=(), inc=None, inc_n=1):
        wl = tuple((s.h, v) for (s, v) in waits if v > 0)
        if inc is not None:
            inc.n += inc_n
            ih, iv = inc.h, inc_n
        else:
            ih, iv = None, 0

        def f(e, build=build, wl=wl, ih=ih, iv=iv):
            for (h, v) in wl:
                e.wait_ge(h, v)
            ins = build(e)
            if ih is not None:
                ins.then_inc(ih, iv)
        self.ops[eng].append(f)

    def wait(self, eng, sem, v):
        if v > 0:
            self.ops[eng].append(lambda e, h=sem.h, v=v: e.wait_ge(h, v))


class Buf:
    __slots__ = ("t", "off", "n", "dt")

    def __init__(self, t, off, n, dt):
        self.t, self.off, self.n, self.dt = t, off, n, dt

    def ap(self):
        a = self.t[:, self.off:self.off + self.n]
        return a.bitcast(BF16) if self.dt == BF16 else a

    def __getitem__(self, idx):
        return self.ap()[idx]


class Alloc:
    def __init__(self, arena_ap, total):
        self.t = arena_ap
        self.free_list = [(0, total)]

    def alloc_f32(self, n):
        n = (n + 15) // 16 * 16
        for i, (o, sz) in enumerate(self.free_list):
            if sz >= n:
                if sz == n:
                    self.free_list.pop(i)
                else:
                    self.free_list[i] = (o + n, sz - n)
                return Buf(self.t, o, n, F32)
        raise RuntimeError(f"arena OOM: need {n}, free={self.free_list}")

    def alloc_bf16(self, cols):
        b = self.alloc_f32((cols + 1) // 2)
        return Buf(self.t, b.off, b.n, BF16)

    def free(self, buf):
        self.free_list.append((buf.off, buf.n))
        self.free_list.sort()
        merged = []
        for (a, b) in self.free_list:
            if merged and merged[-1][0] + merged[-1][1] == a:
                merged[-1][1] += b
            else:
                merged.append([a, b])
        self.free_list = [tuple(x) for x in merged]


def build_nc(dbg=False, stop=99):
    nc = bass.Bass("TRN2", target_bir_lowering=False)
    P = Prog()
    es = ExitStack()

    din = {}
    for name, shape in [
        ("x", [S, D]), ("wq", [D, D]), ("wk", [D, D]), ("wv", [D, D]),
        ("wo", [D, D]), ("omT", [KH, M]),
        ("ln1g", [D]), ("ln1b", [D]), ("w1", [D, FF]), ("b1", [FF]),
        ("w2", [FF, D]), ("b2", [D]), ("ln2g", [D]), ("ln2b", [D]),
        ("wrT", [D, D]), ("wkT", [D, D]), ("wvT", [D, D]),
        ("mur", [D]), ("muk", [D]),
    ]:
        din[name] = nc.dram_tensor(name, shape, F32, kind="ExternalInput").ap()
    out_d = nc.dram_tensor("out", [S, D], F32, kind="ExternalOutput").ap()
    sk = "ExternalOutput" if dbg else "Internal"
    xn_d = nc.dram_tensor("xn_s", [S, D], F32, kind=sk).ap()
    s1_d = nc.dram_tensor("s1_s", [S, D], F32, kind=sk).ap()
    yh_d = nc.dram_tensor("yh_s", [S, D], F32, kind=sk).ap()
    z2_d = nc.dram_tensor("z2_s", [S, D], F32, kind=sk).ap()

    def ut(dram):
        return dram.rearrange("(u p) d -> u p d", p=128)

    ARENA = 47360
    arena_t = es.enter_context(nc.sbuf_tensor("arena", [128, ARENA], F32))
    A = Alloc(arena_t[:], ARENA)

    def sb(name, shape, dt):
        return es.enter_context(nc.sbuf_tensor(name, shape, dt))[:]

    ident = sb("ident", [128, 128], F32)
    identb = sb("identb", [128, 128], BF16)
    ones_row = sb("ones_row", [1, 128], F32)
    ones_rowb = sb("ones_rowb", [1, 128], BF16)
    ones_colb = sb("ones_colb", [128, 1], BF16)
    epsc = sb("epsc", [128, 128], BF16)
    sqrhs = sb("sqrhs", [128, 2], BF16)
    omT_f = sb("omT_f", [64, 256], F32)
    omTs = sb("omTs", [128, 256], BF16)
    omPair = sb("omPair", [128, 512], BF16)
    omUp = sb("omUp", [128, 256], BF16)
    omDn = sb("omDn", [128, 256], BF16)
    gcol1 = sb("gcol1", [128, 8], F32)
    bcol1 = sb("bcol1", [128, 8], F32)
    b1col = sb("b1col", [128, 32], F32)
    murc = sb("murc", [128, 8], F32)
    mukc = sb("mukc", [128, 8], F32)
    omurc = sb("omurc", [128, 8], F32)
    omukc = sb("omukc", [128, 8], F32)
    g1bc = sb("g1bc", [128, 1024], F32)
    g2bc = sb("g2bc", [128, 1024], F32)
    b2lnbc = sb("b2lnbc", [128, 1024], F32)
    btbc = sb("btbc", [128, 1024], F32)
    mvt = sb("mvt", [128, 2], F32)
    bn6 = sb("bn6", [128, 12], F32)
    rstd = sb("rstd", [128, 1], F32)
    nmr = sb("nmr", [128, 1], F32)
    tln = sb("tln", [128, 1], F32)
    maxe = sb("maxe", [128, 48], F32)
    rcpm = sb("rcpm", [128, 48], F32)
    esq = sb("esq", [128, 48], F32)
    ckt = sb("ckt", [128, 48], F32)
    stabt = sb("stabt", [128, 16], F32)
    stab2 = sb("stab2", [128, 16], F32)
    rc1 = sb("rc1", [128, 2], F32)
    cvals = sb("cvals", [128, 8], F32)

    sems = {}

    def sm(name):
        if name not in sems:
            sems[name] = Sem(es.enter_context(nc.semaphore(name)))
        return sems[name]

    dW = sm("dW")
    dA = sm("dA")
    dO = sm("dO")
    dOUT = sm("dOUT")
    bar = sm("bar")
    sv = sm("dve")
    sa = sm("act")
    sp = sm("pe")
    sg = sm("pool")

    def gdma(dst, src, waits=()):
        P.emit("gpsimd", lambda g, d=dst, s=src: g.dma_start(d, s),
               waits=waits, inc=dW, inc_n=16)

    def sdma_in(dst, src, waits=()):
        P.emit("sync", lambda e, d=dst, s=src: e.dma_start(d, s),
               waits=waits, inc=dA, inc_n=16)

    def sdma_out(dst, src, waits=(), final=False):
        P.emit("sync", lambda e, d=dst, s=src: e.dma_start(d, s),
               waits=waits, inc=(dOUT if final else dO), inc_n=16)

    ENGSEM = {"vector": None, "scalar": None, "tensor": None, "gpsimd": None}

    def phase_end(engines=("vector", "scalar", "tensor", "gpsimd")):
        m = {"vector": sv, "scalar": sa, "tensor": sp, "gpsimd": sg}
        return [(m[e], m[e].n) for e in engines]

    def emit_all():
        P.wait("sync", dO, dO.n)
        P.wait("sync", dOUT, dOUT.n)
        with es, nc.Block() as blk:
            @blk.sync
            def _(e):
                for f in P.ops["sync"]:
                    f(e)

            @blk.scalar
            def _(e):
                for f in P.ops["scalar"]:
                    f(e)

            @blk.vector
            def _(e):
                for f in P.ops["vector"]:
                    f(e)

            @blk.tensor
            def _(e):
                for f in P.ops["tensor"]:
                    f(e)

            @blk.gpsimd
            def _(e):
                for f in P.ops["gpsimd"]:
                    f(e)
        return nc

    SER = [False]

    def _ser():
        if not SER[0]:
            return []
        return [(sv, sv.n), (sa, sa.n), (sp, sp.n), (sg, sg.n),
                (dA, dA.n), (dW, dW.n), (dO, dO.n)]

    def V(build, waits=()):
        P.emit("vector", build, waits=list(waits) + [(sv, sv.n)], inc=sv)

    def SC(build, waits=()):
        P.emit("scalar", build, waits=list(waits) + [(sa, sa.n)], inc=sa)

    def T(build, waits=(), noinc=False):
        P.emit("tensor", build, waits=list(waits) + _ser(),
               inc=(None if noinc else sp))

    def G(build, waits=()):
        P.emit("gpsimd", build, waits=list(waits) + [(sg, sg.n)], inc=sg)

    slotw = {}

    class PRot:
        def __init__(self, tensors):
            self.ts = tensors
            self.D = len(tensors)
            self.hist = []
            self.k = 0
            self.fence_pair = None
            self.fence_until = -1

        def acquire(self):
            k = self.k
            self.k += 1
            slot = self.ts[k % self.D]
            w = []
            j = k - (self.D - 1)
            if j >= 0:
                w.append(self.hist[j])
            if k <= self.fence_until and self.fence_pair is not None:
                w.append(self.fence_pair)
            return slot, w

        def release(self, pair):
            assert len(self.hist) == self.k - 1, "release/acquire mismatch"
            self.hist.append(pair)

        def fence(self, pair):
            self.fence_pair = pair
            self.fence_until = self.k + self.D - 2

    fsc_t = sb("fsc", [128, 2], F32)

    def fence_v():
        V(lambda v: v.tensor_copy(fsc_t[:, 0:1], fsc_t[:, 1:2]))
        return (sv, sv.n)

    def fence_sc():
        SC(lambda s: s.copy(fsc_t[:, 0:1], fsc_t[:, 1:2]))
        return (sa, sa.n)

    def accum(pm, key, terms, first_waits, skip=False):
        n = len(terms)
        for i, (l, r) in enumerate(terms):
            w = list(first_waits) if i == 0 else []
            T(lambda t, pm=pm, l=l, r=r, i=i, n=n, skip=skip:
              t.matmul(pm, l, r, start=(i == 0), stop=(i == n - 1),
                       skip_group_check=skip),
              waits=w, noinc=(i != n - 1))

    for j, cval in enumerate([1e-5, 1e-6, -0.5, 0.5, C0, -C0, DPB]):
        V(lambda v, j=j, cval=cval: v.memset(cvals[:, j:j + 1], cval))
        nc.const_aps.register(F32, cval, cvals[:, j:j + 1])
    iota_a = A.alloc_f32(128)
    iota_b = A.alloc_f32(128)
    vrow_b = A.alloc_f32(128)
    rowtmp = A.alloc_f32(1024)
    iota1 = iota_a[:, 0:128]
    iota2 = iota_b[:, 0:128]
    vrow = vrow_b[:, 0:128]
    G(lambda g: g.iota(iota1, [[1, 128]], channel_multiplier=0,
                       allow_small_or_imprecise_dtypes=True))
    G(lambda g: g.iota(iota2, [[0, 128]], channel_multiplier=1,
                       allow_small_or_imprecise_dtypes=True))
    V(lambda v: v.tensor_tensor(ident, iota1, iota2, ALU.is_equal), waits=[(sg, sg.n)])
    V(lambda v: v.tensor_copy(identb, ident))
    V(lambda v: v.memset(ones_row, 1.0))
    V(lambda v: v.memset(ones_rowb, 1.0))
    V(lambda v: v.memset(ones_colb, 1.0))
    V(lambda v: v.memset(epsc, EPHI))
    V(lambda v: v.memset(sqrhs, 0.0))
    V(lambda v: v.memset(sqrhs[0:64, 0:1], SQC))
    V(lambda v: v.memset(sqrhs[64:128, 1:2], SQC))
    gdma(omT_f, din["omT"])
    V(lambda v: v.tensor_scalar(omTs[0:64, :], omT_f, SCALE, 0.0, ALU.mult, ALU.add),
      waits=[(dW, dW.n)])
    V(lambda v: v.tensor_copy(omTs[64:128, :], omTs[0:64, :]))
    V(lambda v: v.memset(omPair, 0.0))
    V(lambda v: v.memset(omUp, 0.0))
    V(lambda v: v.memset(omDn, 0.0))
    V(lambda v: v.tensor_copy(omPair[0:64, 0:256], omTs[0:64, :]))
    V(lambda v: v.tensor_copy(omPair[64:128, 256:512], omTs[0:64, :]))
    V(lambda v: v.tensor_copy(omUp[0:64, :], omTs[0:64, :]))
    V(lambda v: v.tensor_copy(omDn[64:128, :], omTs[0:64, :]))
    ident_done = sv.n

    ps_setup = ExitStack()
    p_vA = ps_setup.enter_context(nc.psum_tensor("p_vA", [128, 64], F32))[:]
    p_vB = ps_setup.enter_context(nc.psum_tensor("p_vB", [128, 64], F32))[:]
    p_bA = ps_setup.enter_context(nc.psum_tensor("p_bA", [128, 512], F32))[:]
    p_bB = ps_setup.enter_context(nc.psum_tensor("p_bB", [128, 512], F32))[:]

    rot_pv = PRot([p_vA, p_vB])
    for j, (src, dstc, nch) in enumerate([
            (din["ln1g"], gcol1, 8), (din["ln1b"], bcol1, 8),
            (din["mur"], murc, 8), (din["muk"], mukc, 8), (din["b1"], b1col, 32)]):
        pvx, war = rot_pv.acquire()
        sdma_in(vrow[0:nch, :], src.rearrange("(c p) -> c p", p=128),
                waits=[(sp, sp.n)])
        T(lambda t, n=nch, pvx=pvx:
          t.transpose(pvx[:, 0:n], vrow[0:n, :], ident[0:n, 0:n]),
          waits=[(dA, dA.n), (sv, ident_done)] + war)
        V(lambda v, d=dstc, n=nch, pvx=pvx: v.tensor_copy(d[:, 0:n], pvx[:, 0:n]),
          waits=[(sp, sp.n)])
        rot_pv.release((sv, sv.n))
    V(lambda v: v.tensor_scalar(omurc, murc, -1.0, 1.0, ALU.mult, ALU.add),
      waits=[(sv, sv.n)])
    V(lambda v: v.tensor_scalar(omukc, mukc, -1.0, 1.0, ALU.mult, ALU.add))

    sdma_in(g1bc[0:1, :], din["ln1g"].rearrange("(o d) -> o d", o=1))
    sdma_in(g2bc[0:1, :], din["ln2g"].rearrange("(o d) -> o d", o=1))
    sdma_in(b2lnbc[0:1, :], din["ln2b"].rearrange("(o d) -> o d", o=1))
    sdma_in(btbc[0:1, :], din["b2"].rearrange("(o d) -> o d", o=1))
    sdma_in(rowtmp[0:1, 0:1024], din["ln1b"].rearrange("(o d) -> o d", o=1))
    V(lambda v: v.tensor_tensor(btbc[0:1, :], btbc[0:1, :], rowtmp[0:1, 0:1024],
                                ALU.add), waits=[(dA, dA.n)])
    rows_ready = sv.n
    rot_pb = PRot([p_bA, p_bB])
    for j, dstbc in enumerate([g1bc, g2bc, b2lnbc, btbc]):
        for half in range(2):
            pvx, war = rot_pb.acquire()
            T(lambda t, d=dstbc, pvx=pvx, hh=half:
              t.matmul(pvx, ones_row, d[0:1, 512 * hh:512 * hh + 512],
                       start=True, stop=True),
              waits=[(sv, rows_ready)] + war)
            V(lambda v, d=dstbc, pvx=pvx, hh=half:
              v.tensor_copy(d[:, 512 * hh:512 * hh + 512], pvx),
              waits=[(sp, sp.n)])
            rot_pb.release((sv, sv.n))
    setup_v, setup_p = sv.n, sp.n
    for b in [iota_a, iota_b, vrow_b, rowtmp]:
        A.free(b)
    ps_setup.close()

    if stop <= 0:
        return emit_all()
    xnfm = A.alloc_bf16(16384)
    wrT_sb = A.alloc_bf16(8192)
    wkT_sb = A.alloc_bf16(8192)
    wvT_sb = A.alloc_bf16(8192)
    for wsrc, wdst in [(din["wrT"], wrT_sb), (din["wkT"], wkT_sb),
                       (din["wvT"], wvT_sb)]:
        gdma(wdst.ap().rearrange("p (c f) -> p c f", c=8),
             wsrc.rearrange("(c p) f -> p c f", p=128),
             waits=[(sv, setup_v), (sp, setup_p)])
    dW_rwkv = dW.n

    ps0 = ExitStack()
    p_tp = [ps0.enter_context(nc.psum_tensor(f"p_tp{i}", [128, 128], BF16))[:]
            for i in range(3)]
    rot_tp = PRot(p_tp)
    xld = [A.alloc_f32(1024) for _ in range(2)]
    xnbf = [A.alloc_bf16(1024) for _ in range(2)]
    xnf32 = [A.alloc_f32(1024) for _ in range(2)]
    act_after = {}
    tp_after = {}
    xn_spill_after = {}
    for u in range(NU):
        sl = u % 2
        sdma_in(xld[sl][:, :], ut(din["x"])[u],
                waits=[(sa, act_after.get(u - 2, 0)),
                       (sv, setup_v if u == 0 else 0), (sp, setup_p if u == 0 else 0)])
        V(lambda v, sl=sl: v.bn_stats(bn6[:, 0:6], xld[sl][:, 0:512]),
          waits=[(dA, dA.n), (sa, act_after.get(u - 1, 0))])
        V(lambda v, sl=sl: v.bn_stats(bn6[:, 6:12], xld[sl][:, 512:1024]))
        V(lambda v: v.bn_aggr(mvt, bn6[:, 0:12]))
        SC(lambda s: s.activation(tln, mvt[:, 1:2], AF.Ln, bias=1e-5, scale=1.0),
           waits=[(sv, sv.n)])
        SC(lambda s: s.activation(rstd, tln, AF.Exp, bias=0.0, scale=-0.5))
        V(lambda v: v.tensor_scalar(nmr, mvt[:, 0:1], rstd[:, 0:1], -1.0,
                                    ALU.mult, ALU.mult), waits=[(sa, sa.n)])
        SC(lambda s, sl=sl: s.activation(xnbf[sl][:, :], xld[sl][:, :], AF.Identity,
                                         bias=nmr[:, 0:1], scale=rstd[:, 0:1]),
           waits=[(sv, sv.n), (sp, tp_after.get(u - 2, 0))])
        xnbf_done = sa.n
        SC(lambda s, sl=sl: s.activation(xnf32[sl][:, :], xld[sl][:, :], AF.Identity,
                                         bias=nmr[:, 0:1], scale=rstd[:, 0:1]),
           waits=[(dO, xn_spill_after.get(u - 2, 0))])
        act_after[u] = sa.n
        sdma_out(ut(xn_d)[u], xnf32[sl][:, :], waits=[(sa, sa.n)])
        xn_spill_after[u] = dO.n
        for c in range(ND):
            pp, war = rot_tp.acquire()
            T(lambda t, sl=sl, c=c, pp=pp:
              t.transpose(pp, xnbf[sl][:, 128 * c:128 * c + 128], identb),
              waits=[(sa, xnbf_done)] + war)
            V(lambda v, c=c, u=u, pp=pp:
              v.tensor_copy(xnfm[:, 2048 * c + 128 * u:2048 * c + 128 * u + 128], pp),
              waits=[(sp, sp.n)])
            rot_tp.release((sv, sv.n))
        tp_after[u] = sp.n
    dO_xn = dO.n
    b0 = phase_end(("vector", "scalar", "tensor"))
    ps0.close()
    for b in xld + xnbf + xnf32:
        A.free(b)

    if stop <= 1:
        return emit_all()
    ps1 = ExitStack()
    p_m = [ps1.enter_context(nc.psum_tensor(f"p_m{i}", [128, 512], F32))[:]
           for i in range(4)]
    p_sq = ps1.enter_context(nc.psum_tensor("p_sq", [128, 512], F32))[:]
    xr = A.alloc_bf16(16384)
    xk = A.alloc_bf16(16384)
    t1bf = A.alloc_bf16(2048)
    r_tm = A.alloc_bf16(16384)
    tanh_b = A.alloc_bf16(1024)
    relu_b = A.alloc_bf16(1024)
    tanh_scr = tanh_b.ap()[:, 0:1024]
    relu_scr = relu_b.ap()[:, 0:1024]

    for c in range(ND):
        blk = xnfm[:, 2048 * c:2048 * c + 2048]
        V(lambda v, blk=blk, c=c:
          v.tensor_scalar(t1bf[:, :], blk, murc[:, c:c + 1], 0.0, ALU.mult, ALU.add),
          waits=list(b0))
        V(lambda v, blk=blk, c=c:
          v.scalar_tensor_tensor(xr[:, 2048 * c + 1:2048 * c + 2048], blk[:, 0:2047],
                                 omurc[:, c:c + 1], t1bf[:, 1:2048], ALU.mult, ALU.add))
        V(lambda v, blk=blk, c=c:
          v.tensor_scalar(xr[:, 2048 * c:2048 * c + 1], blk[:, 0:1], murc[:, c:c + 1],
                          0.0, ALU.mult, ALU.add))
    for c in range(ND):
        blk = xnfm[:, 2048 * c:2048 * c + 2048]
        V(lambda v, blk=blk, c=c:
          v.tensor_scalar(t1bf[:, :], blk, mukc[:, c:c + 1], 0.0, ALU.mult, ALU.add),
          waits=list(b0))
        V(lambda v, blk=blk, c=c:
          v.scalar_tensor_tensor(xk[:, 2048 * c + 1:2048 * c + 2048], blk[:, 0:2047],
                                 omukc[:, c:c + 1], t1bf[:, 1:2048], ALU.mult, ALU.add))
        V(lambda v, blk=blk, c=c:
          v.tensor_scalar(xk[:, 2048 * c:2048 * c + 1], blk[:, 0:1], mukc[:, c:c + 1],
                          0.0, ALU.mult, ALU.add))
    mix_v, mix_p = sv.n, sg.n

    rot_pm = PRot(p_m)
    for u in range(NU):
        for g2 in range(2):
            pm, war = rot_pm.acquire()
            i2 = (2 * u + g2) % 2
            terms = [(xr[:, 2048 * c + 128 * u:2048 * c + 128 * u + 128],
                      wrT_sb[:, 1024 * c + 512 * g2:1024 * c + 512 * g2 + 512])
                     for c in range(ND)]
            accum(pm, None, terms, [(sv, mix_v), (dW, dW_rwkv)] + war)
            SC(lambda s, pm=pm, i2=i2:
               s.activation(tanh_scr[:, 512 * i2:512 * i2 + 512], pm, AF.Tanh,
                            bias=0.0, scale=0.5), waits=[(sp, sp.n)])
            rot_pm.release((sa, sa.n))
            V(lambda v, u=u, g2=g2, i2=i2:
              v.tensor_scalar(r_tm[:, 1024 * u + 512 * g2:1024 * u + 512 * g2 + 512],
                              tanh_scr[:, 512 * i2:512 * i2 + 512], 0.5, 0.5,
                              ALU.mult, ALU.add), waits=[(sa, sa.n)])
    A.free(xr)
    kk = A.alloc_bf16(16384)

    for cq in range(ND):
        for w in range(4):
            pm, war = rot_pm.acquire()
            i2 = (4 * cq + w) % 2
            terms = [(wkT_sb[:, 1024 * c + 128 * cq:1024 * c + 128 * cq + 128],
                      xk[:, 2048 * c + 512 * w:2048 * c + 512 * w + 512])
                     for c in range(ND)]
            accum(pm, None, terms, [(sv, mix_v), (dW, dW_rwkv)] + war)
            SC(lambda s, pm=pm, i2=i2:
               s.activation(relu_scr[:, 512 * i2:512 * i2 + 512], pm,
                            AF.Relu), waits=[(sp, sp.n)])
            rot_pm.release((sa, sa.n))
            V(lambda v, cq=cq, w=w, i2=i2:
              v.tensor_tensor(kk[:, 2048 * cq + 512 * w:2048 * cq + 512 * w + 512],
                              relu_scr[:, 512 * i2:512 * i2 + 512],
                              relu_scr[:, 512 * i2:512 * i2 + 512],
                              ALU.mult), waits=[(sa, sa.n)])
    kk_pe_done = sp.n
    A.free(xk)
    kk_done = sv.n
    rot_pm.fence(fence_sc())

    xld2 = [A.alloc_f32(1024) for _ in range(2)]
    xb_scr = [A.alloc_f32(1024) for _ in range(2)]
    fm_scr = [A.alloc_f32(512) for _ in range(2)]
    s1_scr = [A.alloc_f32(1024) for _ in range(2)]
    pol_after = {}
    s1_dve_after = {}
    s1_spill_after = {}
    for ui, u in enumerate(range(NU)):
        sl = ui % 2
        sdma_in(xld2[sl][:, :], ut(din["x"])[u],
                waits=[(sg, pol_after.get(ui - 2, 0)),
                       (sp, kk_pe_done if ui < 2 else 0)])
        G(lambda g, sl=sl: g.tensor_tensor(xb_scr[sl][:, :], xld2[sl][:, :], b2lnbc,
                                           ALU.add), waits=[(dA, dA.n)])
        pol_after[ui] = sg.n
        for g2 in range(2):
            pm, war = rot_pm.acquire()
            terms = [(kk[:, 2048 * c + 128 * u:2048 * c + 128 * u + 128],
                      wvT_sb[:, 1024 * c + 512 * g2:1024 * c + 512 * g2 + 512])
                     for c in range(ND)]
            accum(pm, None, terms, [(sv, kk_done), (dW, dW_rwkv)] + war)
            V(lambda v, u=u, g2=g2, pm=pm:
              v.tensor_tensor(fm_scr[g2][:, :],
                              r_tm[:, 1024 * u + 512 * g2:1024 * u + 512 * g2 + 512],
                              pm, ALU.mult), waits=[(sp, sp.n)])
            rot_pm.release((sv, sv.n))
            V(lambda v, g2=g2, sl=sl:
              v.tensor_tensor(s1_scr[sl][:, 512 * g2:512 * g2 + 512],
                              xb_scr[sl][:, 512 * g2:512 * g2 + 512],
                              fm_scr[g2][:, :], ALU.add),
              waits=[(sg, pol_after[ui]), (dO, s1_spill_after.get(ui - 2, 0))])
        sdma_out(ut(s1_d)[u], s1_scr[sl][:, :], waits=[(sv, sv.n)])
        s1_spill_after[ui] = dO.n
    dO_s1 = dO.n
    b1e = phase_end()
    for b in ([t1bf, r_tm, kk, wrT_sb, wkT_sb, wvT_sb, tanh_b, relu_b]
              + xld2 + xb_scr + fm_scr + s1_scr):
        A.free(b)

    if stop <= 2:
        if dbg:
            z2v = z2_d.rearrange("(c p) d -> c p d", p=128)
            for u in range(NU):
                gdma(ut(yh_d)[u], r_tm[:, 1024 * u:1024 * u + 1024],
                     waits=list(b1e))
            for c in range(ND):
                gdma(z2v[2 * c], kk[:, 2048 * c:2048 * c + 1024],
                     waits=list(b1e))
                gdma(z2v[2 * c + 1], kk[:, 2048 * c + 1024:2048 * c + 2048],
                     waits=list(b1e))
            P.wait("gpsimd", dW, dW.n)
        return emit_all()
    q_fm = A.alloc_bf16(16384)
    k_fm = A.alloc_bf16(16384)
    v_aug = A.alloc_bf16(16640)
    wq_sb = A.alloc_bf16(8192)
    wk_sb = A.alloc_bf16(8192)
    for wsrc, wdst in [(din["wq"], wq_sb), (din["wk"], wk_sb)]:
        gdma(wdst.ap().rearrange("p (c f) -> p c f", c=8),
             wsrc.rearrange("(c p) f -> p c f", p=128), waits=list(b1e))
    dW_qkv = dW.n

    for u in range(NU):
        V(lambda v, u=u: v.memset(
            v_aug[:, 1040 * u:1040 * u + 1040].rearrange(
                "p (h f) -> p h f", h=16)[:, :, 64:65], 1.0), waits=list(b1e))
    vones_done = sv.n

    rot_pm.fence(fence_v())
    sqq_b = A.alloc_f32(256)
    nsqk_b = A.alloc_f32(256)
    dp_b = A.alloc_f32(256)
    sqq = sqq_b[:, 0:256]
    nsqk = nsqk_b[:, 0:256]
    dp = dp_b[:, 0:256]
    qsq = [A.alloc_bf16(2048) for _ in range(2)]
    first_sq = [True]
    sq_mm_after = {}
    for j, (wsb, dst) in enumerate([(wq_sb, q_fm), (wk_sb, k_fm)]):
        for cq in range(ND):
            for w in range(4):
                pm, war = rot_pm.acquire()
                terms = [(wsb[:, 1024 * c + 128 * cq:1024 * c + 128 * cq + 128],
                          xnfm[:, 2048 * c + 512 * w:2048 * c + 512 * w + 512])
                         for c in range(ND)]
                accum(pm, None, terms, [(dW, dW_qkv)] + list(b1e) + war)
                SC(lambda s, dst=dst, cq=cq, w=w, pm=pm:
                   s.copy(dst[:, 2048 * cq + 512 * w:2048 * cq + 512 * w + 512],
                          pm), waits=[(sp, sp.n)])
                rot_pm.release((sa, sa.n))
            sl = cq % 2
            blk = dst[:, 2048 * cq:2048 * cq + 2048]
            V(lambda v, sl=sl, blk=blk:
              v.tensor_tensor(qsq[sl][:, :], blk, blk, ALU.mult),
              waits=[(sa, sa.n), (sp, sq_mm_after.get((j, cq - 2), 0))])
            sq_dve = sv.n
            for u in range(NU):
                off = 256 * j + 16 * u + 2 * cq
                st = first_sq[0]
                first_sq[0] = False
                T(lambda t, sl=sl, u=u, off=off, st=st:
                  t.matmul(p_sq[:, off:off + 2], qsq[sl][:, 128 * u:128 * u + 128],
                           sqrhs, start=st, stop=True, skip_group_check=True),
                  waits=[(sv, sq_dve)] if u == 0 else [])
            sq_mm_after[(j, cq)] = sp.n
        if j == 0:
            A.free(wq_sb)
            wv_sb = A.alloc_bf16(8192)
            gdma(wv_sb.ap().rearrange("p (c f) -> p c f", c=8),
                 din["wv"].rearrange("(c p) f -> p c f", p=128),
                 waits=[(sp, sp.n), (sa, sa.n)])
            dW_qkv2 = dW.n
    V(lambda v: v.tensor_copy(sqq, p_sq[:, 0:256]), waits=[(sp, sp.n)])
    V(lambda v: v.tensor_scalar(nsqk, p_sq[:, 256:512], -1.0, -C0, ALU.mult, ALU.add))
    fence_v()
    sq_ready = sv.n
    rot_pm.fence(fence_sc())
    for u in range(NU):
        for g2 in range(2):
            pm, war = rot_pm.acquire()
            terms = [(xnfm[:, 2048 * c + 128 * u:2048 * c + 128 * u + 128],
                      wv_sb[:, 1024 * c + 512 * g2:1024 * c + 512 * g2 + 512])
                     for c in range(ND)]
            accum(pm, None, terms, [(dW, dW_qkv2), (sv, vones_done)] + war)
            V(lambda v, u=u, g2=g2, pm=pm:
              v.tensor_copy(
                  v_aug[:, 1040 * u:1040 * u + 1040].rearrange(
                      "p (h f) -> p h f", h=16)[:, 8 * g2:8 * g2 + 8, 0:64],
                  pm.rearrange("p (h f) -> p h f", h=8)),
              waits=[(sp, sp.n)])
            rot_pm.release((sv, sv.n))
    b2a = phase_end(("vector", "scalar", "tensor"))
    ps1.close()
    A.free(xnfm)
    A.free(wk_sb)
    A.free(wv_sb)

    if stop <= 3:
        return emit_all()
    b2b = b2a
    for bq in qsq:
        A.free(bq)
    if stop <= 4:
        return emit_all()
    ps2c = ExitStack()
    p_pk = [ps2c.enter_context(nc.psum_tensor(f"p_pk{i}", [128, 512], F32))[:]
            for i in range(3)]
    rot_pk = PRot(p_pk)
    kvp = [ps2c.enter_context(nc.psum_tensor(f"kvp{i}", [128, 512], F32))[:]
           for i in range(5)]
    kvsb_b = A.alloc_bf16(2080)
    ksrow_b = A.alloc_bf16(1040)
    kvsum_bb = A.alloc_f32(1040)
    vp_b = A.alloc_bf16(260)
    kvsb = kvsb_b.ap()[:, 0:2080]
    ksrow = ksrow_b.ap()
    kvsum_b = kvsum_bb[:, 0:1040]
    vp = vp_b.ap()[:, 0:260]
    ek = [A.alloc_bf16(4096) for _ in range(3)]

    def kvtile(h, mc):
        t = 2 * h + mc
        return kvp[t // 7][:, 65 * (t % 7):65 * (t % 7) + 65], (t % 7 == 0)

    kvmm_done = {}
    rm_done = {}
    LAG = 2
    for step in range(NU + LAG):
        if step < NU:
            u = step
            sl = u % 3
            us = 16 * (u % 3)
            for hp in range(H // 2):
                pk, war = rot_pk.acquire()
                lhs = k_fm[:, 2048 * hp + 128 * u:2048 * hp + 128 * u + 128]
                T(lambda t, pk=pk, lhs=lhs:
                  t.matmul(pk, lhs, omPair, start=True, stop=True),
                  waits=list(b2b) + war)
                SC(lambda s, sl=sl, hp=hp, pk=pk:
                   s.activation(ek[sl][:, 512 * hp:512 * hp + 512], pk, AF.Exp,
                                bias=-C0, scale=1.0),
                   waits=[(sp, sp.n), (sp, kvmm_done.get(u - 3, 0)),
                          (sv, rm_done.get(u - 3, 0))])
                rot_pk.release((sa, sa.n))
                V(lambda v, sl=sl, hp=hp, us=us:
                  v.tensor_reduce(maxe[:, us + 2 * hp:us + 2 * hp + 2],
                                  ek[sl][:, 512 * hp:512 * hp + 512].rearrange(
                                      "p (h m) -> p h m", h=2), AX, ALU.max),
                  waits=[(sa, sa.n)])
            V(lambda v, us=us: v.reciprocal(rcpm[:, us:us + 16],
                                            maxe[:, us:us + 16]))
            SC(lambda s, u=u, us=us:
               s.activation(esq[:, us:us + 16], nsqk[:, 16 * u:16 * u + 16],
                            AF.Exp, bias=C0, scale=1.0), waits=[(sv, sq_ready)])
            V(lambda v, us=us: v.scalar_tensor_tensor(
                ckt[:, us:us + 16], rcpm[:, us:us + 16], 1.0 / 16.0,
                esq[:, us:us + 16], ALU.mult, ALU.mult), waits=[(sa, sa.n)])
            rm_done[u] = sv.n
            for h in range(H):
                for mc in range(2):
                    tile, _ = kvtile(h, mc)
                    T(lambda t, u=u, h=h, tile=tile, mc=mc:
                      t.matmul(tile, epsc,
                               v_aug[:, 1040 * u + 65 * h:1040 * u + 65 * h + 65],
                               start=(u == 0 and (2 * h + mc) % 7 == 0),
                               stop=False, skip_group_check=True),
                      noinc=(not (h == H - 1 and mc == 1)))
        if step >= LAG:
            v_ = step - LAG
            vsl = v_ % 3
            vus = 16 * (v_ % 3)
            for h in range(H):
                vp_i = h % 4
                V(lambda v, v_=v_, h=h, vp_i=vp_i, vus=vus:
                  v.tensor_scalar(vp[:, 65 * vp_i:65 * vp_i + 65],
                                  v_aug[:, 1040 * v_ + 65 * h:1040 * v_ + 65 * h + 65],
                                  ckt[:, vus + h:vus + h + 1], 0.0, ALU.mult, ALU.add),
                  waits=[(sv, rm_done[v_]), slotw.get(("vp", vp_i), (sv, 0))])
                vp_done = sv.n
                for mc in range(2):
                    tile, _ = kvtile(h, mc)
                    T(lambda t, vsl=vsl, h=h, mc=mc, tile=tile, vp_i=vp_i:
                      t.matmul(tile,
                               ek[vsl][:, 256 * h + 128 * mc:256 * h + 128 * mc + 128],
                               vp[:, 65 * vp_i:65 * vp_i + 65],
                               start=False, stop=False, skip_group_check=True),
                      waits=[(sv, vp_done)] if mc == 0 else [], noinc=(mc == 0))
                slotw[("vp", vp_i)] = (sp, sp.n)
            kvmm_done[v_] = sp.n
    for h in range(H):
        for mc in range(2):
            tile, _ = kvtile(h, mc)
            V(lambda v, h=h, mc=mc, tile=tile:
              v.tensor_copy(kvsb[:, 130 * h + 65 * mc:130 * h + 65 * mc + 65], tile),
              waits=[(sp, sp.n)])
    fence_v()
    kv_evac = sv.n
    b2c = phase_end(("vector", "scalar", "tensor"))
    ps2c.close()
    for b in ek:
        A.free(b)
    A.free(k_fm)
    A.free(v_aug)

    if stop <= 5:
        return emit_all()
    ps2k = ExitStack()
    p_ks = [ps2k.enter_context(nc.psum_tensor(f"p_ks{i}", [128, 128], F32))[:]
            for i in range(3)]
    rot_ks = PRot(p_ks)
    for h in range(H):
        pk, war = rot_ks.acquire()
        for mc in range(2):
            T(lambda t, h=h, mc=mc, pk=pk:
              t.matmul(pk[0:1, 0:65], ones_colb,
                       kvsb[:, 130 * h + 65 * mc:130 * h + 65 * mc + 65],
                       start=(mc == 0), stop=(mc == 1)),
              waits=((list(b2c) + war) if mc == 0 else []))
        V(lambda v, h=h, pk=pk:
          v.tensor_copy(ksrow[0:1, 65 * h:65 * h + 65], pk[0:1, 0:65]),
          waits=[(sp, sp.n)])
        rot_ks.release((sv, sv.n))
        ks_dve = sv.n
        pk2, war2 = rot_ks.acquire()
        T(lambda t, h=h, pk2=pk2:
          t.matmul(pk2[:, 0:65], ones_rowb, ksrow[0:1, 65 * h:65 * h + 65],
                   start=True, stop=True), waits=[(sv, ks_dve)] + war2)
        V(lambda v, h=h, pk2=pk2:
          v.tensor_copy(kvsum_b[:, 65 * h:65 * h + 65], pk2[:, 0:65]),
          waits=[(sp, sp.n)])
        rot_ks.release((sv, sv.n))
    fence_v()
    b2k = phase_end(("vector", "tensor"))
    ps2k.close()

    if stop <= 6:
        return emit_all()
    ps2d = ExitStack()
    p_st = [ps2d.enter_context(nc.psum_tensor(f"p_st{i}", [128, 1024], F32))[:]
            for i in range(3)]
    rot_st = PRot(p_st)
    dp_after = {}
    for u in range(NU):
        for grp in range(4):
            pst, war = rot_st.acquire()
            for jp in range(2):
                hp = 2 * grp + jp
                lhs = q_fm[:, 2048 * hp + 128 * u:2048 * hp + 128 * u + 128]
                T(lambda t, lhs=lhs, jp=jp, pst=pst:
                  t.matmul(pst[:, 512 * jp:512 * jp + 512], lhs, omPair,
                           start=True, stop=True, skip_group_check=True),
                  waits=((list(b2k) + war) if jp == 0 else []))
            V(lambda v, grp=grp, pst=pst:
              v.tensor_reduce(stabt[:, 4 * grp:4 * grp + 4],
                              pst.rearrange("p (j m) -> p j m", j=4), AX, ALU.max),
              waits=[(sp, sp.n), (sa, dp_after.get(u - 1, 0))])
            rot_st.release((sv, sv.n))
        V(lambda v, u=u: v.tensor_tensor(stab2, stabt, sqq[:, 16 * u:16 * u + 16],
                                         ALU.add), waits=[(sv, sv.n)])
        SC(lambda s, u=u: s.activation(dp[:, 16 * u:16 * u + 16], stab2, AF.Exp,
                                       bias=DPB, scale=1.0), waits=[(sv, sv.n)])
        dp_after[u] = sa.n
    dp_ready = sa.n
    b2d = phase_end(("vector", "scalar", "tensor"))
    ps2d.close()

    if stop <= 7:
        return emit_all()
    ps2e = ExitStack()
    p_pq = [ps2e.enter_context(nc.psum_tensor(f"p_pq{i}", [128, 512], F32))[:]
            for i in range(2)]
    p_nm = [ps2e.enter_context(nc.psum_tensor(f"p_nm{i}", [128, 130], F32))[:]
            for i in range(3)]
    p_tq = [ps2e.enter_context(nc.psum_tensor(f"p_tq{i}", [128, 128], BF16))[:]
            for i in range(3)]
    rot_pq = PRot(p_pq)
    rot_nm = PRot(p_nm)
    rot_tq = PRot(p_tq)
    eq = [A.alloc_bf16(8192) for _ in range(2)]
    ntm_b = A.alloc_bf16(3 * 128)
    ncr_b = A.alloc_f32(130)
    ntm = ntm_b.ap()[:, 0:3 * 128]
    ncr = ncr_b[:, 0:130]
    numT = A.alloc_bf16(16384)
    wo_sb = A.alloc_bf16(8192)
    gdma(wo_sb.ap().rearrange("p (c f) -> p c f", c=8),
         din["wo"].rearrange("(c p) f -> p c f", p=128), waits=list(b2c))
    dW_wo = dW.n

    num_pe_h = {}
    for hp in range(H // 2):
        slp = hp % 2
        for hh in range(2):
            omhalf = omUp if hh == 0 else omDn
            for mc in range(2):
                for w in range(4):
                    ppq, war = rot_pq.acquire()
                    rhs = q_fm[:, 2048 * hp + 512 * w:2048 * hp + 512 * w + 512]
                    T(lambda t, mc=mc, rhs=rhs, ppq=ppq, omhalf=omhalf:
                      t.matmul(ppq, omhalf[:, 128 * mc:128 * mc + 128], rhs,
                               start=True, stop=True),
                      waits=list(b2d) + war)
                    SC(lambda s, slp=slp, hh=hh, mc=mc, w=w, ppq=ppq:
                       s.activation(eq[slp][:, 4096 * hh + 2048 * mc + 512 * w:
                                            4096 * hh + 2048 * mc + 512 * w + 512],
                                    ppq, AF.Exp, bias=-C0, scale=1.0),
                       waits=[(sp, sp.n), (sp, num_pe_h.get(hp - 2, 0))])
                    rot_pq.release((sa, sa.n))
        eq_done = sa.n
        LAGE = 2
        tq_hist = {}
        for step in range(NU + LAGE):
            if step < NU:
                u = step
                tqslot = u % 3
                pnm, war_nm = rot_nm.acquire()
                for hh in range(2):
                    h = 2 * hp + hh
                    for mc in range(2):
                        T(lambda t, slp=slp, hh=hh, h=h, u=u, mc=mc, pnm=pnm:
                          t.matmul(pnm[:, 65 * hh:65 * hh + 65],
                                   eq[slp][:, 4096 * hh + 2048 * mc + 128 * u:
                                            4096 * hh + 2048 * mc + 128 * u + 128],
                                   kvsb[:, 130 * h + 65 * mc:130 * h + 65 * mc + 65],
                                   start=(hh == 0 and mc == 0), stop=(mc == 1),
                                   skip_group_check=True),
                          waits=(([(sa, eq_done), (sv, kv_evac)] + war_nm)
                                 if (hh == 0 and mc == 0) else []))
                pnm_done = sp.n
                for hh in range(2):
                    h = 2 * hp + hh
                    V(lambda v, h=h, u=u, hh=hh, pnm=pnm:
                      v.scalar_tensor_tensor(ncr[:, 65 * hh:65 * hh + 65],
                                             kvsum_b[:, 65 * h:65 * h + 65],
                                             dp[:, 16 * u + h:16 * u + h + 1],
                                             pnm[:, 65 * hh:65 * hh + 65],
                                             ALU.mult, ALU.add),
                      waits=[(sp, pnm_done), (sa, dp_ready)])
                    V(lambda v, hh=hh:
                      v.reciprocal(rc1[:, hh:hh + 1],
                                   ncr[:, 65 * hh + 64:65 * hh + 65]))
                    V(lambda v, hh=hh, tqslot=tqslot:
                      v.tensor_scalar(ntm[:, 128 * tqslot + 64 * hh:
                                          128 * tqslot + 64 * hh + 64],
                                      ncr[:, 65 * hh:65 * hh + 64],
                                      rc1[:, hh:hh + 1], 0.0, ALU.mult, ALU.add),
                      waits=[slotw.get(("ntm", tqslot), (sv, 0))])
                rot_nm.release((sv, sv.n))
                tq_hist[u] = sv.n
            if step >= LAGE:
                v_ = step - LAGE
                vslot = v_ % 3
                ptq, war_tq = rot_tq.acquire()
                T(lambda t, ptq=ptq, vslot=vslot:
                  t.transpose(ptq, ntm[:, 128 * vslot:128 * vslot + 128], identb),
                  waits=[(sv, tq_hist[v_])] + war_tq)
                tq_done = sp.n
                slotw[("ntm", vslot)] = (sp, tq_done)
                dst = numT[:, 2048 * hp + 128 * v_:2048 * hp + 128 * v_ + 128]
                V(lambda v, dst=dst, ptq=ptq: v.tensor_copy(dst, ptq),
                  waits=[(sp, tq_done)])
                rot_tq.release((sv, sv.n))
        num_pe_h[hp] = sp.n
    fence_v()
    b2e = phase_end(("vector", "scalar", "tensor"))
    ps2e.close()
    for b in eq + [ntm_b, ncr_b, kvsb_b, ksrow_b, kvsum_bb, vp_b,
                   sqq_b, nsqk_b, dp_b]:
        A.free(b)
    A.free(q_fm)

    if stop <= 8:
        return emit_all()
    ps2f = ExitStack()
    p_at = [ps2f.enter_context(nc.psum_tensor(f"p_at{i}", [128, 512], F32))[:]
            for i in range(3)]
    p_tf = [ps2f.enter_context(nc.psum_tensor(f"p_tf{i}", [128, 128], F32))[:]
            for i in range(3)]
    rot_at = PRot(p_at)
    rot_tf = PRot(p_tf)
    streamA = A.alloc_f32(16384)
    y1fm = A.alloc_bf16(16384)
    xnr = [A.alloc_f32(1024) for _ in range(2)]
    z1_dve_after = {}
    yh_spill_after = {}
    for u in range(NU):
        sl = u % 2
        sdma_in(xnr[sl][:, :], ut(xn_d)[u],
                waits=[(dO, dO_xn), (sv, z1_dve_after.get(u - 2, 0)),
                       ] + (list(b2e) if u < 2 else []))
        for g2 in range(2):
            pat, war = rot_at.acquire()
            terms = [(numT[:, 2048 * c + 128 * u:2048 * c + 128 * u + 128],
                      wo_sb[:, 1024 * c + 512 * g2:1024 * c + 512 * g2 + 512])
                     for c in range(ND)]
            accum(pat, None, terms, list(b2e) + [(dW, dW_wo)] + war)
            V(lambda v, u=u, g2=g2, sl=sl, pat=pat:
              v.tensor_tensor(streamA[:, 1024 * u + 512 * g2:1024 * u + 512 * g2 + 512],
                              xnr[sl][:, 512 * g2:512 * g2 + 512], pat, ALU.add),
              waits=[(sp, sp.n), (dA, dA.n)])
            rot_at.release((sv, sv.n))
        zu = streamA[:, 1024 * u:1024 * u + 1024]
        V(lambda v, zu=zu: v.bn_stats(bn6[:, 0:6], zu[:, 0:512]),
          waits=[(sa, dp_after.get(u, 0) if False else 0)])
        V(lambda v, zu=zu: v.bn_stats(bn6[:, 6:12], zu[:, 512:1024]))
        V(lambda v: v.bn_aggr(mvt, bn6[:, 0:12]))
        SC(lambda s: s.activation(tln, mvt[:, 1:2], AF.Ln, bias=1e-6, scale=1.0),
           waits=[(sv, sv.n)])
        SC(lambda s: s.activation(rstd, tln, AF.Exp, bias=0.0, scale=-0.5))
        V(lambda v: v.tensor_scalar(nmr, mvt[:, 0:1], rstd[:, 0:1], -1.0,
                                    ALU.mult, ALU.mult), waits=[(sa, sa.n)])
        SC(lambda s, zu=zu: s.activation(zu, zu, AF.Identity, bias=nmr[:, 0:1],
                                         scale=rstd[:, 0:1]), waits=[(sv, sv.n)])
        sdma_out(ut(yh_d)[u], zu, waits=[(sa, sa.n)])
        yh_spill_after[u] = dO.n
        yhat_done = sa.n
        for c in range(ND):
            ptf, war = rot_tf.acquire()
            T(lambda t, u=u, c=c, ptf=ptf:
              t.transpose(ptf, streamA[:, 1024 * u + 128 * c:1024 * u + 128 * c + 128],
                          ident),
              waits=[(sa, yhat_done)] + war)
            tf_done = sp.n
            dst = y1fm[:, 2048 * c + 128 * u:2048 * c + 128 * u + 128]
            V(lambda v, dst=dst, c=c, ptf=ptf:
              v.tensor_scalar(dst, ptf, gcol1[:, c:c + 1], bcol1[:, c:c + 1],
                              ALU.mult, ALU.add), waits=[(sp, tf_done)])
            rot_tf.release((sv, sv.n))
        z1_dve_after[u] = sv.n
    b2f = phase_end(("vector", "scalar", "tensor"))
    ps2f.close()
    A.free(numT)
    A.free(wo_sb)
    for b in xnr:
        A.free(b)

    if stop <= 9:
        return emit_all()
    fence_v()
    b2g = phase_end(("vector", "scalar", "tensor"))
    A.free(streamA)
    if stop <= 10:
        return emit_all()
    ps4 = ExitStack()
    p_g = [ps4.enter_context(nc.psum_tensor(f"p_g{i}", [128, 512], F32))[:]
           for i in range(3)]
    p_h = [ps4.enter_context(nc.psum_tensor(f"p_h{i}", [128, 512], F32))[:]
           for i in range(3)]
    rot_pg = PRot(p_g)
    rot_ph = PRot(p_h)
    g1 = A.alloc_bf16(16384)
    w2_sb = A.alloc_bf16(32768)
    w1d = [A.alloc_bf16(2048) for _ in range(2)]
    yr = [A.alloc_f32(1024) for _ in range(2)]
    t1s = [A.alloc_f32(512) for _ in range(2)]
    z2s = [A.alloc_f32(1024) for _ in range(3)]
    s1r = [A.alloc_f32(1024) for _ in range(2)]
    y2s = [A.alloc_f32(1024) for _ in range(2)]
    outsb = [A.alloc_f32(1024) for _ in range(2)]
    for part in range(4):
        gdma(w2_sb.ap().rearrange("p (c f) -> p c f", c=32)[:, 8 * part:8 * part + 8, :],
             din["w2"].rearrange("(c p) f -> p c f", p=128)[:, 8 * part:8 * part + 8, :],
             waits=list(b2g))
    dW_w2 = dW.n

    w1_mm_after = {}
    g1_pe_after = {}
    z2_dve_after = {}
    fin_v = {}
    fin_g = {}
    fin_a = {}
    for t4 in range(4):
        for fp2 in range(NFF // 2):
            sl = fp2 % 2
            gdma(w1d[sl].ap().rearrange("p (c f) -> p c f", c=8),
                 din["w1"].rearrange("(c p) f -> p c f",
                                     p=128)[:, :, 256 * fp2:256 * fp2 + 256],
                 waits=list(b2g) + [(sp, w1_mm_after.get((t4, fp2 - 2), 0))])
            dW_ff = dW.n
            for fh in range(2):
                ff = 2 * fp2 + fh
                pg, war = rot_pg.acquire()
                terms = [(w1d[sl][:, 256 * c + 128 * fh:256 * c + 128 * fh + 128],
                          y1fm[:, 2048 * c + 512 * t4:2048 * c + 512 * t4 + 512])
                         for c in range(ND)]
                accum(pg, None, terms,
                      [(dW, dW_ff), (sp, g1_pe_after.get(t4 - 1, 0))] + war)
                SC(lambda s, ff=ff, pg=pg:
                   s.activation(g1[:, 512 * ff:512 * ff + 512], pg, AF.Gelu,
                                bias=b1col[:, ff:ff + 1], scale=1.0),
                   waits=[(sp, sp.n), (sp, g1_pe_after.get(t4 - 1, 0))])
                rot_pg.release((sa, sa.n))
            w1_mm_after[(t4, fp2)] = sp.n
        g1_ready = sa.n
        for uu in range(4):
            u = 4 * t4 + uu
            sl = u % 2
            zsl = u % 3
            sdma_in(yr[sl][:, :], ut(yh_d)[u],
                    waits=[(dO, yh_spill_after[u]), (sv, z2_dve_after.get(u - 2, 0))]
                          + (list(b2g) if u < 2 else []))
            sdma_in(s1r[sl][:, :], ut(s1_d)[u],
                    waits=[(dO, dO_s1), (sv, fin_v.get(u - 2, 0)),
                           (sg, fin_g.get(u - 2, 0))])
            for g2 in range(2):
                ph, war_h = rot_ph.acquire()
                terms = [(g1[:, 512 * ff + 128 * uu:512 * ff + 128 * uu + 128],
                          w2_sb[:, 1024 * ff + 512 * g2:1024 * ff + 512 * g2 + 512])
                         for ff in range(NFF)]
                accum(ph, None, terms, [(sa, g1_ready), (dW, dW_w2)] + war_h)
                V(lambda v, sl=sl, g2=g2:
                  v.tensor_tensor(t1s[g2][:, :], yr[sl][:, 512 * g2:512 * g2 + 512],
                                  g1bc[:, 512 * g2:512 * g2 + 512], ALU.mult),
                  waits=[(dA, dA.n), (sg, sg.n)])
                G(lambda g, g2=g2:
                  g.tensor_tensor(t1s[g2][:, :], t1s[g2][:, :],
                                  btbc[:, 512 * g2:512 * g2 + 512], ALU.add),
                  waits=[(sv, sv.n)])
                V(lambda v, zsl=zsl, g2=g2, ph=ph:
                  v.scalar_tensor_tensor(z2s[zsl][:, 512 * g2:512 * g2 + 512],
                                         t1s[g2][:, :], 1.0, ph, ALU.mult, ALU.add),
                  waits=[(sg, sg.n), (sp, sp.n), (sa, fin_a.get(u - 3, 0)),
                         (sv, fin_v.get(u - 3, 0))])
                rot_ph.release((sv, sv.n))
            if dbg:
                sdma_out(ut(z2_d)[u], z2s[zsl][:, :], waits=[(sv, sv.n)])
            z2_dve_after[u] = sv.n
            V(lambda v, zsl=zsl: v.bn_stats(bn6[:, 0:6], z2s[zsl][:, 0:512]))
            V(lambda v, zsl=zsl: v.bn_stats(bn6[:, 6:12], z2s[zsl][:, 512:1024]))
            V(lambda v: v.bn_aggr(mvt, bn6[:, 0:12]))
            SC(lambda s: s.activation(tln, mvt[:, 1:2], AF.Ln, bias=1e-6, scale=1.0),
               waits=[(sv, sv.n)])
            SC(lambda s: s.activation(rstd, tln, AF.Exp, bias=0.0, scale=-0.5))
            V(lambda v: v.tensor_scalar(nmr, mvt[:, 0:1], rstd[:, 0:1], -1.0,
                                        ALU.mult, ALU.mult), waits=[(sa, sa.n)])
            SC(lambda s, sl=sl, zsl=zsl:
               s.activation(y2s[sl][:, :], z2s[zsl][:, :], AF.Identity,
                            bias=nmr[:, 0:1], scale=rstd[:, 0:1]),
               waits=[(sv, sv.n), (sv, fin_v.get(u - 2, 0)),
                      (sg, fin_g.get(u - 2, 0))])
            y2_done = sa.n
            fin_a[u] = sa.n
            V(lambda v, sl=sl: v.tensor_tensor(outsb[sl][:, 0:512],
                                               y2s[sl][:, 0:512],
                                               g2bc[:, 0:512], ALU.mult),
              waits=[(sa, y2_done), (dOUT, 16 * max(0, u - 1))])
            V(lambda v, sl=sl: v.tensor_tensor(outsb[sl][:, 0:512],
                                               outsb[sl][:, 0:512],
                                               s1r[sl][:, 0:512], ALU.add),
              waits=[(dA, dA.n)])
            G(lambda g, sl=sl: g.tensor_tensor(outsb[sl][:, 512:1024],
                                               y2s[sl][:, 512:1024],
                                               g2bc[:, 512:1024], ALU.mult),
              waits=[(sa, y2_done), (dOUT, 16 * max(0, u - 1))])
            G(lambda g, sl=sl: g.tensor_tensor(outsb[sl][:, 512:1024],
                                               outsb[sl][:, 512:1024],
                                               s1r[sl][:, 512:1024], ALU.add),
              waits=[(dA, dA.n)])
            fin_v[u] = sv.n
            fin_g[u] = sg.n
            sdma_out(ut(out_d)[u], outsb[sl][:, :],
                     waits=[(sv, fin_v[u]), (sg, fin_g[u])], final=True)
        g1_pe_after[t4] = sp.n

    return emit_all()


def prep_inputs(inputs):
    f = np.float32
    com = {
        "wq": np.ascontiguousarray(np.asarray(inputs["L0_attn_wq"], f).reshape(D, D)),
        "wk": np.ascontiguousarray(np.asarray(inputs["L0_attn_wk"], f).reshape(D, D)),
        "wv": np.ascontiguousarray(np.asarray(inputs["L0_attn_wv"], f).reshape(D, D)),
        "wo": np.ascontiguousarray(np.asarray(inputs["L0_attn_wo"], f)),
        "omT": np.ascontiguousarray(np.asarray(inputs["L0_attn_omega"], f).T),
        "ln1g": np.ascontiguousarray(np.asarray(inputs["L0_attn_ln1_g"], f)),
        "ln1b": np.ascontiguousarray(np.asarray(inputs["L0_attn_ln1_b"], f)),
        "w1": np.ascontiguousarray(np.asarray(inputs["L0_attn_w1"], f)),
        "b1": np.ascontiguousarray(np.asarray(inputs["L0_attn_b1"], f)),
        "w2": np.ascontiguousarray(np.asarray(inputs["L0_attn_w2"], f)),
        "b2": np.ascontiguousarray(np.asarray(inputs["L0_attn_b2"], f)),
        "ln2g": np.ascontiguousarray(np.asarray(inputs["L0_attn_ln2_g"], f)),
        "ln2b": np.ascontiguousarray(np.asarray(inputs["L0_attn_ln2_b"], f)),
        "wrT": np.ascontiguousarray(np.asarray(inputs["L0_ffn_W_r"], f).T),
        "wkT": np.ascontiguousarray(np.asarray(inputs["L0_ffn_W_k"], f).T),
        "wvT": np.ascontiguousarray(np.asarray(inputs["L0_ffn_W_v"], f).T),
        "mur": np.ascontiguousarray(np.asarray(inputs["L0_ffn_mu_r"], f)),
        "muk": np.ascontiguousarray(np.asarray(inputs["L0_ffn_mu_k"], f)),
    }
    x = np.asarray(inputs["L0_attn_x"], f)
    return x, com


def kernel(**inputs):
    x, com = prep_inputs(inputs)
    nc = build_nc(dbg=False)
    in_maps = []
    for b in range(8):
        m = dict(com)
        m["x"] = np.ascontiguousarray(x[b])
        in_maps.append(m)
    res = run_bass_kernel_spmd(nc, in_maps, core_ids=list(range(8)))
    return np.stack([np.asarray(r["out"], np.float32) for r in res.results], 0)


if __name__ == "__main__":
    nc = build_nc()
    print("built ok, instructions:", len(list(nc.all_instructions())))


# version 13
# speedup vs baseline: 1.0335x; 1.0335x over previous
import math
import numpy as np
import ml_dtypes
import concourse.bass as bass
import concourse.mybir as mybir
from concourse.bass_utils import run_bass_kernel_spmd
from contextlib import ExitStack

F32 = mybir.dt.float32
BF16 = mybir.dt.bfloat16
AF = mybir.ActivationFunctionType
ALU = mybir.AluOpType
AX = mybir.AxisListType.X

S, D, H, KH, M, FF = 2048, 1024, 16, 64, 256, 4096
NU, ND, NFF = S // 128, D // 128, FF // 128
SCALE = 64.0 ** -0.25
SQC = 0.5 * SCALE * SCALE
C0 = 25.0
DPB = math.log(16.0 * 1e-6) - C0
EPHI = 1e-6
ENGS = ("sync", "scalar", "vector", "tensor", "gpsimd")


class Sem:
    __slots__ = ("h", "n")

    def __init__(self, h):
        self.h = h
        self.n = 0


class Prog:
    def __init__(self):
        self.ops = {e: [] for e in ENGS}

    def emit(self, eng, build, waits=(), inc=None, inc_n=1):
        wl = tuple((s.h, v) for (s, v) in waits if v > 0)
        if inc is not None:
            inc.n += inc_n
            ih, iv = inc.h, inc_n
        else:
            ih, iv = None, 0

        def f(e, build=build, wl=wl, ih=ih, iv=iv):
            for (h, v) in wl:
                e.wait_ge(h, v)
            ins = build(e)
            if ih is not None:
                ins.then_inc(ih, iv)
        self.ops[eng].append(f)

    def wait(self, eng, sem, v):
        if v > 0:
            self.ops[eng].append(lambda e, h=sem.h, v=v: e.wait_ge(h, v))


class Buf:
    __slots__ = ("t", "off", "n", "dt")

    def __init__(self, t, off, n, dt):
        self.t, self.off, self.n, self.dt = t, off, n, dt

    def ap(self):
        a = self.t[:, self.off:self.off + self.n]
        return a.bitcast(BF16) if self.dt == BF16 else a

    def __getitem__(self, idx):
        return self.ap()[idx]


class Alloc:
    def __init__(self, arena_ap, total):
        self.t = arena_ap
        self.free_list = [(0, total)]

    def alloc_f32(self, n):
        n = (n + 15) // 16 * 16
        for i, (o, sz) in enumerate(self.free_list):
            if sz >= n:
                if sz == n:
                    self.free_list.pop(i)
                else:
                    self.free_list[i] = (o + n, sz - n)
                return Buf(self.t, o, n, F32)
        raise RuntimeError(f"arena OOM: need {n}, free={self.free_list}")

    def alloc_bf16(self, cols):
        b = self.alloc_f32((cols + 1) // 2)
        return Buf(self.t, b.off, b.n, BF16)

    def free(self, buf):
        self.free_list.append((buf.off, buf.n))
        self.free_list.sort()
        merged = []
        for (a, b) in self.free_list:
            if merged and merged[-1][0] + merged[-1][1] == a:
                merged[-1][1] += b
            else:
                merged.append([a, b])
        self.free_list = [tuple(x) for x in merged]


def build_nc(dbg=False, stop=99):
    nc = bass.Bass("TRN2", target_bir_lowering=False)
    P = Prog()
    es = ExitStack()

    din = {}
    for name, shape in [
        ("x", [S, D]), ("wq", [D, D]), ("wk", [D, D]), ("wv", [D, D]),
        ("wo", [D, D]), ("omT", [KH, M]),
        ("ln1g", [D]), ("ln1b", [D]), ("w1", [D, FF]), ("b1", [FF]),
        ("w2", [FF, D]), ("b2", [D]), ("ln2g", [D]), ("ln2b", [D]),
        ("wrT", [D, D]), ("wkT", [D, D]), ("wvT", [D, D]),
        ("mur", [D]), ("muk", [D]),
    ]:
        din[name] = nc.dram_tensor(name, shape, F32, kind="ExternalInput").ap()
    out_d = nc.dram_tensor("out", [S, D], F32, kind="ExternalOutput").ap()
    sk = "ExternalOutput" if dbg else "Internal"
    xn_d = nc.dram_tensor("xn_s", [S, D], F32, kind=sk).ap()
    s1_d = nc.dram_tensor("s1_s", [S, D], F32, kind=sk).ap()
    yh_d = nc.dram_tensor("yh_s", [S, D], F32, kind=sk).ap()
    z2_d = nc.dram_tensor("z2_s", [S, D], F32, kind=sk).ap()

    def ut(dram):
        return dram.rearrange("(u p) d -> u p d", p=128)

    ARENA = 47360
    arena_t = es.enter_context(nc.sbuf_tensor("arena", [128, ARENA], F32))
    A = Alloc(arena_t[:], ARENA)

    def sb(name, shape, dt):
        return es.enter_context(nc.sbuf_tensor(name, shape, dt))[:]

    ident = sb("ident", [128, 128], F32)
    identb = sb("identb", [128, 128], BF16)
    ones_row = sb("ones_row", [1, 128], F32)
    ones_rowb = sb("ones_rowb", [1, 128], BF16)
    ones_colb = sb("ones_colb", [128, 1], BF16)
    epsc = sb("epsc", [128, 128], BF16)
    sqrhs = sb("sqrhs", [128, 2], BF16)
    omT_f = sb("omT_f", [64, 256], F32)
    omTs = sb("omTs", [128, 256], BF16)
    omPair = sb("omPair", [128, 512], BF16)
    omUp = sb("omUp", [128, 256], BF16)
    omDn = sb("omDn", [128, 256], BF16)
    gcol1 = sb("gcol1", [128, 8], F32)
    bcol1 = sb("bcol1", [128, 8], F32)
    b1col = sb("b1col", [128, 32], F32)
    murc = sb("murc", [128, 8], F32)
    mukc = sb("mukc", [128, 8], F32)
    omurc = sb("omurc", [128, 8], F32)
    omukc = sb("omukc", [128, 8], F32)
    g1bc = sb("g1bc", [128, 1024], F32)
    g2bc = sb("g2bc", [128, 1024], F32)
    b2lnbc = sb("b2lnbc", [128, 1024], F32)
    btbc = sb("btbc", [128, 1024], F32)
    mvt = sb("mvt", [128, 2], F32)
    bn6 = sb("bn6", [128, 12], F32)
    rstd = sb("rstd", [128, 1], F32)
    nmr = sb("nmr", [128, 1], F32)
    tln = sb("tln", [128, 1], F32)
    maxe = sb("maxe", [128, 48], F32)
    rcpm = sb("rcpm", [128, 48], F32)
    esq = sb("esq", [128, 48], F32)
    ckt = sb("ckt", [128, 48], F32)
    stabt = sb("stabt", [128, 16], F32)
    stab2 = sb("stab2", [128, 16], F32)
    rc1 = sb("rc1", [128, 6], F32)
    cvals = sb("cvals", [128, 8], F32)

    sems = {}

    def sm(name):
        if name not in sems:
            sems[name] = Sem(es.enter_context(nc.semaphore(name)))
        return sems[name]

    dW = sm("dW")
    dA = sm("dA")
    dO = sm("dO")
    dOUT = sm("dOUT")
    bar = sm("bar")
    sv = sm("dve")
    sa = sm("act")
    sp = sm("pe")
    sg = sm("pool")

    def gdma(dst, src, waits=()):
        P.emit("gpsimd", lambda g, d=dst, s=src: g.dma_start(d, s),
               waits=waits, inc=dW, inc_n=16)

    def sdma_in(dst, src, waits=()):
        P.emit("sync", lambda e, d=dst, s=src: e.dma_start(d, s),
               waits=waits, inc=dA, inc_n=16)

    def sdma_out(dst, src, waits=(), final=False):
        P.emit("sync", lambda e, d=dst, s=src: e.dma_start(d, s),
               waits=waits, inc=(dOUT if final else dO), inc_n=16)

    ENGSEM = {"vector": None, "scalar": None, "tensor": None, "gpsimd": None}

    def phase_end(engines=("vector", "scalar", "tensor", "gpsimd")):
        m = {"vector": sv, "scalar": sa, "tensor": sp, "gpsimd": sg}
        return [(m[e], m[e].n) for e in engines]

    def emit_all():
        P.wait("sync", dO, dO.n)
        P.wait("sync", dOUT, dOUT.n)
        with es, nc.Block() as blk:
            @blk.sync
            def _(e):
                for f in P.ops["sync"]:
                    f(e)

            @blk.scalar
            def _(e):
                for f in P.ops["scalar"]:
                    f(e)

            @blk.vector
            def _(e):
                for f in P.ops["vector"]:
                    f(e)

            @blk.tensor
            def _(e):
                for f in P.ops["tensor"]:
                    f(e)

            @blk.gpsimd
            def _(e):
                for f in P.ops["gpsimd"]:
                    f(e)
        return nc

    SER = [False]

    def _ser():
        if not SER[0]:
            return []
        return [(sv, sv.n), (sa, sa.n), (sp, sp.n), (sg, sg.n),
                (dA, dA.n), (dW, dW.n), (dO, dO.n)]

    def V(build, waits=()):
        P.emit("vector", build, waits=list(waits) + [(sv, sv.n)], inc=sv)

    def SC(build, waits=()):
        P.emit("scalar", build, waits=list(waits) + [(sa, sa.n)], inc=sa)

    def T(build, waits=(), noinc=False):
        P.emit("tensor", build, waits=list(waits) + _ser(),
               inc=(None if noinc else sp))

    def G(build, waits=()):
        P.emit("gpsimd", build, waits=list(waits) + [(sg, sg.n)], inc=sg)

    slotw = {}

    class PRot:
        def __init__(self, tensors):
            self.ts = tensors
            self.D = len(tensors)
            self.hist = []
            self.k = 0
            self.fence_pair = None
            self.fence_until = -1

        def acquire(self):
            k = self.k
            self.k += 1
            slot = self.ts[k % self.D]
            w = []
            j = k - (self.D - 1)
            if j >= 0:
                w.append(self.hist[j])
            if k <= self.fence_until and self.fence_pair is not None:
                w.append(self.fence_pair)
            return slot, w

        def release(self, pair):
            assert len(self.hist) == self.k - 1, "release/acquire mismatch"
            self.hist.append(pair)

        def fence(self, pair):
            self.fence_pair = pair
            self.fence_until = self.k + self.D - 2

    fsc_t = sb("fsc", [128, 2], F32)

    def fence_v():
        V(lambda v: v.tensor_copy(fsc_t[:, 0:1], fsc_t[:, 1:2]))
        return (sv, sv.n)

    def fence_sc():
        SC(lambda s: s.copy(fsc_t[:, 0:1], fsc_t[:, 1:2]))
        return (sa, sa.n)

    def accum(pm, key, terms, first_waits, skip=False):
        n = len(terms)
        for i, (l, r) in enumerate(terms):
            w = list(first_waits) if i == 0 else []
            T(lambda t, pm=pm, l=l, r=r, i=i, n=n, skip=skip:
              t.matmul(pm, l, r, start=(i == 0), stop=(i == n - 1),
                       skip_group_check=skip),
              waits=w, noinc=(i != n - 1))

    for j, cval in enumerate([1e-5, 1e-6, -0.5, 0.5, C0, -C0, DPB]):
        V(lambda v, j=j, cval=cval: v.memset(cvals[:, j:j + 1], cval))
        nc.const_aps.register(F32, cval, cvals[:, j:j + 1])
    iota_a = A.alloc_f32(128)
    iota_b = A.alloc_f32(128)
    vrow_b = A.alloc_f32(128)
    rowtmp = A.alloc_f32(1024)
    iota1 = iota_a[:, 0:128]
    iota2 = iota_b[:, 0:128]
    vrow = vrow_b[:, 0:128]
    G(lambda g: g.iota(iota1, [[1, 128]], channel_multiplier=0,
                       allow_small_or_imprecise_dtypes=True))
    G(lambda g: g.iota(iota2, [[0, 128]], channel_multiplier=1,
                       allow_small_or_imprecise_dtypes=True))
    V(lambda v: v.tensor_tensor(ident, iota1, iota2, ALU.is_equal), waits=[(sg, sg.n)])
    V(lambda v: v.tensor_copy(identb, ident))
    V(lambda v: v.memset(ones_row, 1.0))
    V(lambda v: v.memset(ones_rowb, 1.0))
    V(lambda v: v.memset(ones_colb, 1.0))
    V(lambda v: v.memset(epsc, EPHI))
    V(lambda v: v.memset(sqrhs, 0.0))
    V(lambda v: v.memset(sqrhs[0:64, 0:1], SQC))
    V(lambda v: v.memset(sqrhs[64:128, 1:2], SQC))
    gdma(omT_f, din["omT"])
    V(lambda v: v.tensor_scalar(omTs[0:64, :], omT_f, SCALE, 0.0, ALU.mult, ALU.add),
      waits=[(dW, dW.n)])
    V(lambda v: v.tensor_copy(omTs[64:128, :], omTs[0:64, :]))
    V(lambda v: v.memset(omPair, 0.0))
    V(lambda v: v.memset(omUp, 0.0))
    V(lambda v: v.memset(omDn, 0.0))
    V(lambda v: v.tensor_copy(omPair[0:64, 0:256], omTs[0:64, :]))
    V(lambda v: v.tensor_copy(omPair[64:128, 256:512], omTs[0:64, :]))
    V(lambda v: v.tensor_copy(omUp[0:64, :], omTs[0:64, :]))
    V(lambda v: v.tensor_copy(omDn[64:128, :], omTs[0:64, :]))
    ident_done = sv.n

    ps_setup = ExitStack()
    p_vA = ps_setup.enter_context(nc.psum_tensor("p_vA", [128, 64], F32))[:]
    p_vB = ps_setup.enter_context(nc.psum_tensor("p_vB", [128, 64], F32))[:]
    p_bA = ps_setup.enter_context(nc.psum_tensor("p_bA", [128, 512], F32))[:]
    p_bB = ps_setup.enter_context(nc.psum_tensor("p_bB", [128, 512], F32))[:]

    rot_pv = PRot([p_vA, p_vB])
    for j, (src, dstc, nch) in enumerate([
            (din["ln1g"], gcol1, 8), (din["ln1b"], bcol1, 8),
            (din["mur"], murc, 8), (din["muk"], mukc, 8), (din["b1"], b1col, 32)]):
        pvx, war = rot_pv.acquire()
        sdma_in(vrow[0:nch, :], src.rearrange("(c p) -> c p", p=128),
                waits=[(sp, sp.n)])
        T(lambda t, n=nch, pvx=pvx:
          t.transpose(pvx[:, 0:n], vrow[0:n, :], ident[0:n, 0:n]),
          waits=[(dA, dA.n), (sv, ident_done)] + war)
        V(lambda v, d=dstc, n=nch, pvx=pvx: v.tensor_copy(d[:, 0:n], pvx[:, 0:n]),
          waits=[(sp, sp.n)])
        rot_pv.release((sv, sv.n))
    V(lambda v: v.tensor_scalar(omurc, murc, -1.0, 1.0, ALU.mult, ALU.add),
      waits=[(sv, sv.n)])
    V(lambda v: v.tensor_scalar(omukc, mukc, -1.0, 1.0, ALU.mult, ALU.add))

    sdma_in(g1bc[0:1, :], din["ln1g"].rearrange("(o d) -> o d", o=1))
    sdma_in(g2bc[0:1, :], din["ln2g"].rearrange("(o d) -> o d", o=1))
    sdma_in(b2lnbc[0:1, :], din["ln2b"].rearrange("(o d) -> o d", o=1))
    sdma_in(btbc[0:1, :], din["b2"].rearrange("(o d) -> o d", o=1))
    sdma_in(rowtmp[0:1, 0:1024], din["ln1b"].rearrange("(o d) -> o d", o=1))
    V(lambda v: v.tensor_tensor(btbc[0:1, :], btbc[0:1, :], rowtmp[0:1, 0:1024],
                                ALU.add), waits=[(dA, dA.n)])
    rows_ready = sv.n
    rot_pb = PRot([p_bA, p_bB])
    for j, dstbc in enumerate([g1bc, g2bc, b2lnbc, btbc]):
        for half in range(2):
            pvx, war = rot_pb.acquire()
            T(lambda t, d=dstbc, pvx=pvx, hh=half:
              t.matmul(pvx, ones_row, d[0:1, 512 * hh:512 * hh + 512],
                       start=True, stop=True),
              waits=[(sv, rows_ready)] + war)
            V(lambda v, d=dstbc, pvx=pvx, hh=half:
              v.tensor_copy(d[:, 512 * hh:512 * hh + 512], pvx),
              waits=[(sp, sp.n)])
            rot_pb.release((sv, sv.n))
    setup_v, setup_p = sv.n, sp.n
    for b in [iota_a, iota_b, vrow_b, rowtmp]:
        A.free(b)
    ps_setup.close()

    if stop <= 0:
        return emit_all()
    xnfm = A.alloc_bf16(16384)
    wrT_sb = A.alloc_bf16(8192)
    wkT_sb = A.alloc_bf16(8192)
    wvT_sb = A.alloc_bf16(8192)
    for wsrc, wdst in [(din["wrT"], wrT_sb), (din["wkT"], wkT_sb),
                       (din["wvT"], wvT_sb)]:
        gdma(wdst.ap().rearrange("p (c f) -> p c f", c=8),
             wsrc.rearrange("(c p) f -> p c f", p=128),
             waits=[(sv, setup_v), (sp, setup_p)])
    dW_rwkv = dW.n

    ps0 = ExitStack()
    p_tp = [ps0.enter_context(nc.psum_tensor(f"p_tp{i}", [128, 128], BF16))[:]
            for i in range(3)]
    rot_tp = PRot(p_tp)
    xld = [A.alloc_f32(1024) for _ in range(2)]
    xnbf = [A.alloc_bf16(1024) for _ in range(2)]
    xnf32 = [A.alloc_f32(1024) for _ in range(2)]
    act_after = {}
    tp_after = {}
    xn_spill_after = {}
    for u in range(NU):
        sl = u % 2
        sdma_in(xld[sl][:, :], ut(din["x"])[u],
                waits=[(sa, act_after.get(u - 2, 0)),
                       (sv, setup_v if u == 0 else 0), (sp, setup_p if u == 0 else 0)])
        V(lambda v, sl=sl: v.bn_stats(bn6[:, 0:6], xld[sl][:, 0:512]),
          waits=[(dA, dA.n), (sa, act_after.get(u - 1, 0))])
        V(lambda v, sl=sl: v.bn_stats(bn6[:, 6:12], xld[sl][:, 512:1024]))
        V(lambda v: v.bn_aggr(mvt, bn6[:, 0:12]))
        SC(lambda s: s.activation(tln, mvt[:, 1:2], AF.Ln, bias=1e-5, scale=1.0),
           waits=[(sv, sv.n)])
        SC(lambda s: s.activation(rstd, tln, AF.Exp, bias=0.0, scale=-0.5))
        V(lambda v: v.tensor_scalar(nmr, mvt[:, 0:1], rstd[:, 0:1], -1.0,
                                    ALU.mult, ALU.mult), waits=[(sa, sa.n)])
        SC(lambda s, sl=sl: s.activation(xnbf[sl][:, :], xld[sl][:, :], AF.Identity,
                                         bias=nmr[:, 0:1], scale=rstd[:, 0:1]),
           waits=[(sv, sv.n), (sp, tp_after.get(u - 2, 0))])
        xnbf_done = sa.n
        SC(lambda s, sl=sl: s.activation(xnf32[sl][:, :], xld[sl][:, :], AF.Identity,
                                         bias=nmr[:, 0:1], scale=rstd[:, 0:1]),
           waits=[(dO, xn_spill_after.get(u - 2, 0))])
        act_after[u] = sa.n
        sdma_out(ut(xn_d)[u], xnf32[sl][:, :], waits=[(sa, sa.n)])
        xn_spill_after[u] = dO.n
        for c in range(ND):
            pp, war = rot_tp.acquire()
            T(lambda t, sl=sl, c=c, pp=pp:
              t.transpose(pp, xnbf[sl][:, 128 * c:128 * c + 128], identb),
              waits=[(sa, xnbf_done)] + war)
            V(lambda v, c=c, u=u, pp=pp:
              v.tensor_copy(xnfm[:, 2048 * c + 128 * u:2048 * c + 128 * u + 128], pp),
              waits=[(sp, sp.n)])
            rot_tp.release((sv, sv.n))
        tp_after[u] = sp.n
    dO_xn = dO.n
    b0 = phase_end(("vector", "scalar", "tensor"))
    ps0.close()
    for b in xld + xnbf + xnf32:
        A.free(b)

    if stop <= 1:
        return emit_all()
    ps1 = ExitStack()
    p_m = [ps1.enter_context(nc.psum_tensor(f"p_m{i}", [128, 512], F32))[:]
           for i in range(4)]
    p_sq = ps1.enter_context(nc.psum_tensor("p_sq", [128, 512], F32))[:]
    xr = A.alloc_bf16(16384)
    xk = A.alloc_bf16(16384)
    t1bf = A.alloc_bf16(2048)
    r_tm = A.alloc_bf16(16384)
    tanh_b = A.alloc_bf16(1024)
    relu_b = A.alloc_bf16(1024)
    tanh_scr = tanh_b.ap()[:, 0:1024]
    relu_scr = relu_b.ap()[:, 0:1024]

    for c in range(ND):
        blk = xnfm[:, 2048 * c:2048 * c + 2048]
        V(lambda v, blk=blk, c=c:
          v.tensor_scalar(t1bf[:, :], blk, murc[:, c:c + 1], 0.0, ALU.mult, ALU.add),
          waits=list(b0))
        V(lambda v, blk=blk, c=c:
          v.scalar_tensor_tensor(xr[:, 2048 * c + 1:2048 * c + 2048], blk[:, 0:2047],
                                 omurc[:, c:c + 1], t1bf[:, 1:2048], ALU.mult, ALU.add))
        V(lambda v, blk=blk, c=c:
          v.tensor_scalar(xr[:, 2048 * c:2048 * c + 1], blk[:, 0:1], murc[:, c:c + 1],
                          0.0, ALU.mult, ALU.add))
    for c in range(ND):
        blk = xnfm[:, 2048 * c:2048 * c + 2048]
        V(lambda v, blk=blk, c=c:
          v.tensor_scalar(t1bf[:, :], blk, mukc[:, c:c + 1], 0.0, ALU.mult, ALU.add),
          waits=list(b0))
        V(lambda v, blk=blk, c=c:
          v.scalar_tensor_tensor(xk[:, 2048 * c + 1:2048 * c + 2048], blk[:, 0:2047],
                                 omukc[:, c:c + 1], t1bf[:, 1:2048], ALU.mult, ALU.add))
        V(lambda v, blk=blk, c=c:
          v.tensor_scalar(xk[:, 2048 * c:2048 * c + 1], blk[:, 0:1], mukc[:, c:c + 1],
                          0.0, ALU.mult, ALU.add))
    mix_v, mix_p = sv.n, sg.n

    rot_pm = PRot(p_m)
    for u in range(NU):
        for g2 in range(2):
            pm, war = rot_pm.acquire()
            i2 = (2 * u + g2) % 2
            terms = [(xr[:, 2048 * c + 128 * u:2048 * c + 128 * u + 128],
                      wrT_sb[:, 1024 * c + 512 * g2:1024 * c + 512 * g2 + 512])
                     for c in range(ND)]
            accum(pm, None, terms, [(sv, mix_v), (dW, dW_rwkv)] + war)
            SC(lambda s, pm=pm, i2=i2:
               s.activation(tanh_scr[:, 512 * i2:512 * i2 + 512], pm, AF.Tanh,
                            bias=0.0, scale=0.5), waits=[(sp, sp.n)])
            rot_pm.release((sa, sa.n))
            V(lambda v, u=u, g2=g2, i2=i2:
              v.tensor_scalar(r_tm[:, 1024 * u + 512 * g2:1024 * u + 512 * g2 + 512],
                              tanh_scr[:, 512 * i2:512 * i2 + 512], 0.5, 0.5,
                              ALU.mult, ALU.add), waits=[(sa, sa.n)])
    A.free(xr)
    kk = A.alloc_bf16(16384)

    for cq in range(ND):
        for w in range(4):
            pm, war = rot_pm.acquire()
            i2 = (4 * cq + w) % 2
            terms = [(wkT_sb[:, 1024 * c + 128 * cq:1024 * c + 128 * cq + 128],
                      xk[:, 2048 * c + 512 * w:2048 * c + 512 * w + 512])
                     for c in range(ND)]
            accum(pm, None, terms, [(sv, mix_v), (dW, dW_rwkv)] + war)
            SC(lambda s, pm=pm, i2=i2:
               s.activation(relu_scr[:, 512 * i2:512 * i2 + 512], pm,
                            AF.Relu), waits=[(sp, sp.n)])
            rot_pm.release((sa, sa.n))
            V(lambda v, cq=cq, w=w, i2=i2:
              v.tensor_tensor(kk[:, 2048 * cq + 512 * w:2048 * cq + 512 * w + 512],
                              relu_scr[:, 512 * i2:512 * i2 + 512],
                              relu_scr[:, 512 * i2:512 * i2 + 512],
                              ALU.mult), waits=[(sa, sa.n)])
    kk_pe_done = sp.n
    A.free(xk)
    kk_done = sv.n
    rot_pm.fence(fence_sc())

    xld2 = [A.alloc_f32(1024) for _ in range(2)]
    xb_scr = [A.alloc_f32(1024) for _ in range(2)]
    fm_scr = [A.alloc_f32(512) for _ in range(2)]
    s1_scr = [A.alloc_f32(1024) for _ in range(2)]
    pol_after = {}
    s1_dve_after = {}
    s1_spill_after = {}
    for ui, u in enumerate(range(NU)):
        sl = ui % 2
        sdma_in(xld2[sl][:, :], ut(din["x"])[u],
                waits=[(sg, pol_after.get(ui - 2, 0)),
                       (sp, kk_pe_done if ui < 2 else 0)])
        G(lambda g, sl=sl: g.tensor_tensor(xb_scr[sl][:, :], xld2[sl][:, :], b2lnbc,
                                           ALU.add), waits=[(dA, dA.n)])
        pol_after[ui] = sg.n
        for g2 in range(2):
            pm, war = rot_pm.acquire()
            terms = [(kk[:, 2048 * c + 128 * u:2048 * c + 128 * u + 128],
                      wvT_sb[:, 1024 * c + 512 * g2:1024 * c + 512 * g2 + 512])
                     for c in range(ND)]
            accum(pm, None, terms, [(sv, kk_done), (dW, dW_rwkv)] + war)
            V(lambda v, u=u, g2=g2, pm=pm:
              v.tensor_tensor(fm_scr[g2][:, :],
                              r_tm[:, 1024 * u + 512 * g2:1024 * u + 512 * g2 + 512],
                              pm, ALU.mult), waits=[(sp, sp.n)])
            rot_pm.release((sv, sv.n))
            V(lambda v, g2=g2, sl=sl:
              v.tensor_tensor(s1_scr[sl][:, 512 * g2:512 * g2 + 512],
                              xb_scr[sl][:, 512 * g2:512 * g2 + 512],
                              fm_scr[g2][:, :], ALU.add),
              waits=[(sg, pol_after[ui]), (dO, s1_spill_after.get(ui - 2, 0))])
        sdma_out(ut(s1_d)[u], s1_scr[sl][:, :], waits=[(sv, sv.n)])
        s1_spill_after[ui] = dO.n
    dO_s1 = dO.n
    b1e = phase_end()
    for b in ([t1bf, r_tm, kk, wrT_sb, wkT_sb, wvT_sb, tanh_b, relu_b]
              + xld2 + xb_scr + fm_scr + s1_scr):
        A.free(b)

    if stop <= 2:
        if dbg:
            z2v = z2_d.rearrange("(c p) d -> c p d", p=128)
            for u in range(NU):
                gdma(ut(yh_d)[u], r_tm[:, 1024 * u:1024 * u + 1024],
                     waits=list(b1e))
            for c in range(ND):
                gdma(z2v[2 * c], kk[:, 2048 * c:2048 * c + 1024],
                     waits=list(b1e))
                gdma(z2v[2 * c + 1], kk[:, 2048 * c + 1024:2048 * c + 2048],
                     waits=list(b1e))
            P.wait("gpsimd", dW, dW.n)
        return emit_all()
    q_fm = A.alloc_bf16(16384)
    k_fm = A.alloc_bf16(16384)
    v_aug = A.alloc_bf16(16640)
    wq_sb = A.alloc_bf16(8192)
    wk_sb = A.alloc_bf16(8192)
    for wsrc, wdst in [(din["wq"], wq_sb), (din["wk"], wk_sb)]:
        gdma(wdst.ap().rearrange("p (c f) -> p c f", c=8),
             wsrc.rearrange("(c p) f -> p c f", p=128), waits=list(b1e))
    dW_qkv = dW.n

    for u in range(NU):
        V(lambda v, u=u: v.memset(
            v_aug[:, 1040 * u:1040 * u + 1040].rearrange(
                "p (h f) -> p h f", h=16)[:, :, 64:65], 1.0), waits=list(b1e))
    vones_done = sv.n

    rot_pm.fence(fence_v())
    sqq_b = A.alloc_f32(256)
    nsqk_b = A.alloc_f32(256)
    dp_b = A.alloc_f32(256)
    sqq = sqq_b[:, 0:256]
    nsqk = nsqk_b[:, 0:256]
    dp = dp_b[:, 0:256]
    qsq = [A.alloc_bf16(2048) for _ in range(2)]
    first_sq = [True]
    sq_mm_after = {}
    for j, (wsb, dst) in enumerate([(wq_sb, q_fm), (wk_sb, k_fm)]):
        for cq in range(ND):
            for w in range(4):
                pm, war = rot_pm.acquire()
                terms = [(wsb[:, 1024 * c + 128 * cq:1024 * c + 128 * cq + 128],
                          xnfm[:, 2048 * c + 512 * w:2048 * c + 512 * w + 512])
                         for c in range(ND)]
                accum(pm, None, terms, [(dW, dW_qkv)] + list(b1e) + war)
                SC(lambda s, dst=dst, cq=cq, w=w, pm=pm:
                   s.copy(dst[:, 2048 * cq + 512 * w:2048 * cq + 512 * w + 512],
                          pm), waits=[(sp, sp.n)])
                rot_pm.release((sa, sa.n))
            sl = cq % 2
            blk = dst[:, 2048 * cq:2048 * cq + 2048]
            V(lambda v, sl=sl, blk=blk:
              v.tensor_tensor(qsq[sl][:, :], blk, blk, ALU.mult),
              waits=[(sa, sa.n), (sp, sq_mm_after.get((j, cq - 2), 0))])
            sq_dve = sv.n
            for u in range(NU):
                off = 256 * j + 16 * u + 2 * cq
                st = first_sq[0]
                first_sq[0] = False
                T(lambda t, sl=sl, u=u, off=off, st=st:
                  t.matmul(p_sq[:, off:off + 2], qsq[sl][:, 128 * u:128 * u + 128],
                           sqrhs, start=st, stop=True, skip_group_check=True),
                  waits=[(sv, sq_dve)] if u == 0 else [])
            sq_mm_after[(j, cq)] = sp.n
        if j == 0:
            A.free(wq_sb)
            wv_sb = A.alloc_bf16(8192)
            gdma(wv_sb.ap().rearrange("p (c f) -> p c f", c=8),
                 din["wv"].rearrange("(c p) f -> p c f", p=128),
                 waits=[(sp, sp.n), (sa, sa.n)])
            dW_qkv2 = dW.n
    V(lambda v: v.tensor_copy(sqq, p_sq[:, 0:256]), waits=[(sp, sp.n)])
    V(lambda v: v.tensor_scalar(nsqk, p_sq[:, 256:512], -1.0, -C0, ALU.mult, ALU.add))
    fence_v()
    sq_ready = sv.n
    rot_pm.fence(fence_sc())
    for u in range(NU):
        for g2 in range(2):
            pm, war = rot_pm.acquire()
            terms = [(xnfm[:, 2048 * c + 128 * u:2048 * c + 128 * u + 128],
                      wv_sb[:, 1024 * c + 512 * g2:1024 * c + 512 * g2 + 512])
                     for c in range(ND)]
            accum(pm, None, terms, [(dW, dW_qkv2), (sv, vones_done)] + war)
            V(lambda v, u=u, g2=g2, pm=pm:
              v.tensor_copy(
                  v_aug[:, 1040 * u:1040 * u + 1040].rearrange(
                      "p (h f) -> p h f", h=16)[:, 8 * g2:8 * g2 + 8, 0:64],
                  pm.rearrange("p (h f) -> p h f", h=8)),
              waits=[(sp, sp.n)])
            rot_pm.release((sv, sv.n))
    b2a = phase_end(("vector", "scalar", "tensor"))
    ps1.close()
    A.free(xnfm)
    A.free(wk_sb)
    A.free(wv_sb)

    if stop <= 3:
        return emit_all()
    b2b = b2a
    for bq in qsq:
        A.free(bq)
    if stop <= 4:
        return emit_all()
    ps2c = ExitStack()
    p_pk = [ps2c.enter_context(nc.psum_tensor(f"p_pk{i}", [128, 512], F32))[:]
            for i in range(3)]
    rot_pk = PRot(p_pk)
    kvp = [ps2c.enter_context(nc.psum_tensor(f"kvp{i}", [128, 512], F32))[:]
           for i in range(5)]
    kvsb_b = A.alloc_bf16(2080)
    ksrow_b = A.alloc_bf16(1040)
    kvsum_bb = A.alloc_f32(1040)
    vp_b = A.alloc_bf16(260)
    kvsb = kvsb_b.ap()[:, 0:2080]
    ksrow = ksrow_b.ap()
    kvsum_b = kvsum_bb[:, 0:1040]
    vp = vp_b.ap()[:, 0:260]
    ek = [A.alloc_bf16(4096) for _ in range(3)]

    def kvtile(h, mc):
        t = 2 * h + mc
        return kvp[t // 7][:, 65 * (t % 7):65 * (t % 7) + 65], (t % 7 == 0)

    kvmm_done = {}
    rm_done = {}
    LAG = 2
    for step in range(NU + LAG):
        if step < NU:
            u = step
            sl = u % 3
            us = 16 * (u % 3)
            for hp in range(H // 2):
                pk, war = rot_pk.acquire()
                lhs = k_fm[:, 2048 * hp + 128 * u:2048 * hp + 128 * u + 128]
                T(lambda t, pk=pk, lhs=lhs:
                  t.matmul(pk, lhs, omPair, start=True, stop=True),
                  waits=list(b2b) + war)
                SC(lambda s, sl=sl, hp=hp, pk=pk:
                   s.activation(ek[sl][:, 512 * hp:512 * hp + 512], pk, AF.Exp,
                                bias=-C0, scale=1.0),
                   waits=[(sp, sp.n), (sp, kvmm_done.get(u - 3, 0)),
                          (sv, rm_done.get(u - 3, 0))])
                rot_pk.release((sa, sa.n))
                V(lambda v, sl=sl, hp=hp, us=us:
                  v.tensor_reduce(maxe[:, us + 2 * hp:us + 2 * hp + 2],
                                  ek[sl][:, 512 * hp:512 * hp + 512].rearrange(
                                      "p (h m) -> p h m", h=2), AX, ALU.max),
                  waits=[(sa, sa.n)])
            V(lambda v, us=us: v.reciprocal(rcpm[:, us:us + 16],
                                            maxe[:, us:us + 16]))
            SC(lambda s, u=u, us=us:
               s.activation(esq[:, us:us + 16], nsqk[:, 16 * u:16 * u + 16],
                            AF.Exp, bias=C0, scale=1.0), waits=[(sv, sq_ready)])
            V(lambda v, us=us: v.scalar_tensor_tensor(
                ckt[:, us:us + 16], rcpm[:, us:us + 16], 1.0 / 16.0,
                esq[:, us:us + 16], ALU.mult, ALU.mult), waits=[(sa, sa.n)])
            rm_done[u] = sv.n
            for h in range(H):
                for mc in range(2):
                    tile, _ = kvtile(h, mc)
                    T(lambda t, u=u, h=h, tile=tile, mc=mc:
                      t.matmul(tile, epsc,
                               v_aug[:, 1040 * u + 65 * h:1040 * u + 65 * h + 65],
                               start=(u == 0 and (2 * h + mc) % 7 == 0),
                               stop=False, skip_group_check=True),
                      noinc=(not (h == H - 1 and mc == 1)))
        if step >= LAG:
            v_ = step - LAG
            vsl = v_ % 3
            vus = 16 * (v_ % 3)
            for h in range(H):
                vp_i = h % 4
                if h % 2 == 0:
                    V(lambda v, v_=v_, h=h, vp_i=vp_i, vus=vus:
                      v.tensor_scalar(vp[:, 65 * vp_i:65 * vp_i + 65],
                                      v_aug[:, 1040 * v_ + 65 * h:
                                            1040 * v_ + 65 * h + 65],
                                      ckt[:, vus + h:vus + h + 1], 0.0,
                                      ALU.mult, ALU.add),
                      waits=[(sv, rm_done[v_]), slotw.get(("vp", vp_i), (sv, 0))])
                    vp_done = (sv, sv.n)
                else:
                    SC(lambda s, v_=v_, h=h, vp_i=vp_i, vus=vus:
                       s.activation(vp[:, 65 * vp_i:65 * vp_i + 65],
                                    v_aug[:, 1040 * v_ + 65 * h:
                                          1040 * v_ + 65 * h + 65],
                                    AF.Identity, bias=0.0,
                                    scale=ckt[:, vus + h:vus + h + 1]),
                       waits=[(sv, rm_done[v_]), slotw.get(("vp", vp_i), (sv, 0))])
                    vp_done = (sa, sa.n)
                for mc in range(2):
                    tile, _ = kvtile(h, mc)
                    T(lambda t, vsl=vsl, h=h, mc=mc, tile=tile, vp_i=vp_i:
                      t.matmul(tile,
                               ek[vsl][:, 256 * h + 128 * mc:256 * h + 128 * mc + 128],
                               vp[:, 65 * vp_i:65 * vp_i + 65],
                               start=False, stop=False, skip_group_check=True),
                      waits=[vp_done] if mc == 0 else [], noinc=(mc == 0))
                slotw[("vp", vp_i)] = (sp, sp.n)
            kvmm_done[v_] = sp.n
    for h in range(H):
        for mc in range(2):
            tile, _ = kvtile(h, mc)
            V(lambda v, h=h, mc=mc, tile=tile:
              v.tensor_copy(kvsb[:, 130 * h + 65 * mc:130 * h + 65 * mc + 65], tile),
              waits=[(sp, sp.n)])
    fence_v()
    kv_evac = sv.n
    b2c = phase_end(("vector", "scalar", "tensor"))
    ps2c.close()
    for b in ek:
        A.free(b)
    A.free(k_fm)
    A.free(v_aug)

    if stop <= 5:
        return emit_all()
    ps2k = ExitStack()
    p_ks = [ps2k.enter_context(nc.psum_tensor(f"p_ks{i}", [128, 128], F32))[:]
            for i in range(3)]
    rot_ks = PRot(p_ks)
    for h in range(H):
        pk, war = rot_ks.acquire()
        for mc in range(2):
            T(lambda t, h=h, mc=mc, pk=pk:
              t.matmul(pk[0:1, 0:65], ones_colb,
                       kvsb[:, 130 * h + 65 * mc:130 * h + 65 * mc + 65],
                       start=(mc == 0), stop=(mc == 1)),
              waits=((list(b2c) + war) if mc == 0 else []))
        V(lambda v, h=h, pk=pk:
          v.tensor_copy(ksrow[0:1, 65 * h:65 * h + 65], pk[0:1, 0:65]),
          waits=[(sp, sp.n)])
        rot_ks.release((sv, sv.n))
        ks_dve = sv.n
        pk2, war2 = rot_ks.acquire()
        T(lambda t, h=h, pk2=pk2:
          t.matmul(pk2[:, 0:65], ones_rowb, ksrow[0:1, 65 * h:65 * h + 65],
                   start=True, stop=True), waits=[(sv, ks_dve)] + war2)
        V(lambda v, h=h, pk2=pk2:
          v.tensor_copy(kvsum_b[:, 65 * h:65 * h + 65], pk2[:, 0:65]),
          waits=[(sp, sp.n)])
        rot_ks.release((sv, sv.n))
    fence_v()
    b2k = phase_end(("vector", "tensor"))
    ps2k.close()

    if stop <= 6:
        return emit_all()
    ps2d = ExitStack()
    p_st = [ps2d.enter_context(nc.psum_tensor(f"p_st{i}", [128, 1024], F32))[:]
            for i in range(3)]
    rot_st = PRot(p_st)
    dp_after = {}
    for u in range(NU):
        for grp in range(4):
            pst, war = rot_st.acquire()
            for jp in range(2):
                hp = 2 * grp + jp
                lhs = q_fm[:, 2048 * hp + 128 * u:2048 * hp + 128 * u + 128]
                T(lambda t, lhs=lhs, jp=jp, pst=pst:
                  t.matmul(pst[:, 512 * jp:512 * jp + 512], lhs, omPair,
                           start=True, stop=True, skip_group_check=True),
                  waits=((list(b2k) + war) if jp == 0 else []))
            V(lambda v, grp=grp, pst=pst:
              v.tensor_reduce(stabt[:, 4 * grp:4 * grp + 4],
                              pst.rearrange("p (j m) -> p j m", j=4), AX, ALU.max),
              waits=[(sp, sp.n), (sa, dp_after.get(u - 1, 0))])
            rot_st.release((sv, sv.n))
        V(lambda v, u=u: v.tensor_tensor(stab2, stabt, sqq[:, 16 * u:16 * u + 16],
                                         ALU.add), waits=[(sv, sv.n)])
        SC(lambda s, u=u: s.activation(dp[:, 16 * u:16 * u + 16], stab2, AF.Exp,
                                       bias=DPB, scale=1.0), waits=[(sv, sv.n)])
        dp_after[u] = sa.n
    dp_ready = sa.n
    b2d = phase_end(("vector", "scalar", "tensor"))
    ps2d.close()

    if stop <= 7:
        return emit_all()
    ps2e = ExitStack()
    p_pq = [ps2e.enter_context(nc.psum_tensor(f"p_pq{i}", [128, 512], F32))[:]
            for i in range(2)]
    p_nm = [ps2e.enter_context(nc.psum_tensor(f"p_nm{i}", [128, 130], F32))[:]
            for i in range(3)]
    p_tq = [ps2e.enter_context(nc.psum_tensor(f"p_tq{i}", [128, 128], BF16))[:]
            for i in range(3)]
    rot_pq = PRot(p_pq)
    rot_nm = PRot(p_nm)
    rot_tq = PRot(p_tq)
    eq = [A.alloc_bf16(8192) for _ in range(2)]
    ntm_b = A.alloc_bf16(6 * 128)
    ncr_b = A.alloc_f32(392)
    ntm = ntm_b.ap()[:, 0:6 * 128]
    ncr = ncr_b[:, 0:392]
    numT = A.alloc_bf16(16384)
    wo_sb = A.alloc_bf16(8192)
    gdma(wo_sb.ap().rearrange("p (c f) -> p c f", c=8),
         din["wo"].rearrange("(c p) f -> p c f", p=128), waits=list(b2c))
    dW_wo = dW.n

    num_pe_h = {}
    eq_done = {}

    def eq_jobs(hp):
        slp = hp % 2
        jobs = []
        for hh in range(2):
            omhalf = omUp if hh == 0 else omDn
            for mc in range(2):
                for w in range(4):
                    def job(hp=hp, slp=slp, hh=hh, mc=mc, w=w, omhalf=omhalf):
                        ppq, war = rot_pq.acquire()
                        rhs = q_fm[:, 2048 * hp + 512 * w:2048 * hp + 512 * w + 512]
                        T(lambda t, mc=mc, rhs=rhs, ppq=ppq, omhalf=omhalf:
                          t.matmul(ppq, omhalf[:, 128 * mc:128 * mc + 128], rhs,
                                   start=True, stop=True),
                          waits=list(b2d) + war + [(sp, num_pe_h.get(hp - 2, 0))])
                        SC(lambda s, slp=slp, hh=hh, mc=mc, w=w, ppq=ppq:
                           s.activation(
                               eq[slp][:, 4096 * hh + 2048 * mc + 512 * w:
                                       4096 * hh + 2048 * mc + 512 * w + 512],
                               ppq, AF.Exp, bias=-C0, scale=1.0),
                           waits=[(sp, sp.n)])
                        rot_pq.release((sa, sa.n))
                    jobs.append(job)
        return jobs

    LAGE = 5
    tq_hist = {}
    for hp in range(H // 2):
        slp = hp % 2
        for j in eq_jobs(hp):
            j()
        eq_done[hp] = sa.n
        for step in range(NU + LAGE):
            if step < NU:
                u = step
                tqslot = u % 6
                pnm, war_nm = rot_nm.acquire()
                for hh in range(2):
                    h = 2 * hp + hh
                    for mc in range(2):
                        T(lambda t, slp=slp, hh=hh, h=h, u=u, mc=mc, pnm=pnm:
                          t.matmul(pnm[:, 65 * hh:65 * hh + 65],
                                   eq[slp][:, 4096 * hh + 2048 * mc + 128 * u:
                                            4096 * hh + 2048 * mc + 128 * u + 128],
                                   kvsb[:, 130 * h + 65 * mc:130 * h + 65 * mc + 65],
                                   start=(hh == 0 and mc == 0), stop=(mc == 1),
                                   skip_group_check=True),
                          waits=(([(sa, eq_done[hp]), (sv, kv_evac)] + war_nm)
                                 if (hh == 0 and mc == 0) else []))
                pnm_done = sp.n
                nsl = 131 * (u % 3)
                rsl = 2 * (u % 3)
                for hh in range(2):
                    h = 2 * hp + hh
                    V(lambda v, h=h, u=u, hh=hh, pnm=pnm, nsl=nsl:
                      v.scalar_tensor_tensor(ncr[:, nsl + 65 * hh:nsl + 65 * hh + 65],
                                             kvsum_b[:, 65 * h:65 * h + 65],
                                             dp[:, 16 * u + h:16 * u + h + 1],
                                             pnm[:, 65 * hh:65 * hh + 65],
                                             ALU.mult, ALU.add),
                      waits=[(sp, pnm_done), (sa, dp_ready),
                             (sa, tq_hist.get(16 * hp + u - 3, 0))])
                V(lambda v, nsl=nsl, rsl=rsl:
                  v.reciprocal(rc1[:, rsl:rsl + 1], ncr[:, nsl + 64:nsl + 65]))
                V(lambda v, nsl=nsl, rsl=rsl:
                  v.reciprocal(rc1[:, rsl + 1:rsl + 2], ncr[:, nsl + 129:nsl + 130]))
                rot_nm.release((sv, sv.n))
                div_dve = sv.n
                for hh in range(2):
                    SC(lambda s, hh=hh, tqslot=tqslot, nsl=nsl, rsl=rsl:
                       s.activation(ntm[:, 128 * tqslot + 64 * hh:
                                        128 * tqslot + 64 * hh + 64],
                                    ncr[:, nsl + 65 * hh:nsl + 65 * hh + 64],
                                    AF.Identity, bias=0.0,
                                    scale=rc1[:, rsl + hh:rsl + hh + 1]),
                       waits=[(sv, div_dve), slotw.get(("ntm", tqslot), (sv, 0))])
                tq_hist[16 * hp + u] = sa.n
            if step >= LAGE:
                v_ = step - LAGE
                vslot = v_ % 6
                ptq, war_tq = rot_tq.acquire()
                T(lambda t, ptq=ptq, vslot=vslot:
                  t.transpose(ptq, ntm[:, 128 * vslot:128 * vslot + 128], identb),
                  waits=[(sa, tq_hist[16 * hp + v_])] + war_tq)
                tq_done = sp.n
                slotw[("ntm", vslot)] = (sp, tq_done)
                dst = numT[:, 2048 * hp + 128 * v_:2048 * hp + 128 * v_ + 128]
                SC(lambda s, dst=dst, ptq=ptq: s.copy(dst, ptq),
                   waits=[(sp, tq_done)])
                rot_tq.release((sa, sa.n))
        num_pe_h[hp] = sp.n
    fence_v()
    b2e = phase_end(("vector", "scalar", "tensor"))
    ps2e.close()
    for b in eq + [ntm_b, ncr_b, kvsb_b, ksrow_b, kvsum_bb, vp_b,
                   sqq_b, nsqk_b, dp_b]:
        A.free(b)
    A.free(q_fm)

    if stop <= 8:
        return emit_all()
    ps2f = ExitStack()
    p_at = [ps2f.enter_context(nc.psum_tensor(f"p_at{i}", [128, 512], F32))[:]
            for i in range(3)]
    p_tf = [ps2f.enter_context(nc.psum_tensor(f"p_tf{i}", [128, 128], F32))[:]
            for i in range(3)]
    rot_at = PRot(p_at)
    rot_tf = PRot(p_tf)
    streamA = A.alloc_f32(16384)
    y1fm = A.alloc_bf16(16384)
    xnr = [A.alloc_f32(1024) for _ in range(2)]
    z1_dve_after = {}
    yh_spill_after = {}
    for u in range(NU):
        sl = u % 2
        sdma_in(xnr[sl][:, :], ut(xn_d)[u],
                waits=[(dO, dO_xn), (sv, z1_dve_after.get(u - 2, 0)),
                       ] + (list(b2e) if u < 2 else []))
        for g2 in range(2):
            pat, war = rot_at.acquire()
            terms = [(numT[:, 2048 * c + 128 * u:2048 * c + 128 * u + 128],
                      wo_sb[:, 1024 * c + 512 * g2:1024 * c + 512 * g2 + 512])
                     for c in range(ND)]
            accum(pat, None, terms, list(b2e) + [(dW, dW_wo)] + war)
            V(lambda v, u=u, g2=g2, sl=sl, pat=pat:
              v.tensor_tensor(streamA[:, 1024 * u + 512 * g2:1024 * u + 512 * g2 + 512],
                              xnr[sl][:, 512 * g2:512 * g2 + 512], pat, ALU.add),
              waits=[(sp, sp.n), (dA, dA.n)])
            rot_at.release((sv, sv.n))
        zu = streamA[:, 1024 * u:1024 * u + 1024]
        V(lambda v, zu=zu: v.bn_stats(bn6[:, 0:6], zu[:, 0:512]),
          waits=[(sa, dp_after.get(u, 0) if False else 0)])
        V(lambda v, zu=zu: v.bn_stats(bn6[:, 6:12], zu[:, 512:1024]))
        V(lambda v: v.bn_aggr(mvt, bn6[:, 0:12]))
        SC(lambda s: s.activation(tln, mvt[:, 1:2], AF.Ln, bias=1e-6, scale=1.0),
           waits=[(sv, sv.n)])
        SC(lambda s: s.activation(rstd, tln, AF.Exp, bias=0.0, scale=-0.5))
        V(lambda v: v.tensor_scalar(nmr, mvt[:, 0:1], rstd[:, 0:1], -1.0,
                                    ALU.mult, ALU.mult), waits=[(sa, sa.n)])
        SC(lambda s, zu=zu: s.activation(zu, zu, AF.Identity, bias=nmr[:, 0:1],
                                         scale=rstd[:, 0:1]), waits=[(sv, sv.n)])
        sdma_out(ut(yh_d)[u], zu, waits=[(sa, sa.n)])
        yh_spill_after[u] = dO.n
        yhat_done = sa.n
        for c in range(ND):
            ptf, war = rot_tf.acquire()
            T(lambda t, u=u, c=c, ptf=ptf:
              t.transpose(ptf, streamA[:, 1024 * u + 128 * c:1024 * u + 128 * c + 128],
                          ident),
              waits=[(sa, yhat_done)] + war)
            tf_done = sp.n
            dst = y1fm[:, 2048 * c + 128 * u:2048 * c + 128 * u + 128]
            V(lambda v, dst=dst, c=c, ptf=ptf:
              v.tensor_scalar(dst, ptf, gcol1[:, c:c + 1], bcol1[:, c:c + 1],
                              ALU.mult, ALU.add), waits=[(sp, tf_done)])
            rot_tf.release((sv, sv.n))
        z1_dve_after[u] = sv.n
    b2f = phase_end(("vector", "scalar", "tensor"))
    ps2f.close()
    A.free(numT)
    A.free(wo_sb)
    for b in xnr:
        A.free(b)

    if stop <= 9:
        return emit_all()
    fence_v()
    b2g = phase_end(("vector", "scalar", "tensor"))
    A.free(streamA)
    if stop <= 10:
        return emit_all()
    ps4 = ExitStack()
    p_g = [ps4.enter_context(nc.psum_tensor(f"p_g{i}", [128, 512], F32))[:]
           for i in range(3)]
    p_h = [ps4.enter_context(nc.psum_tensor(f"p_h{i}", [128, 512], F32))[:]
           for i in range(3)]
    rot_pg = PRot(p_g)
    rot_ph = PRot(p_h)
    g1 = A.alloc_bf16(16384)
    w2_sb = A.alloc_bf16(32768)
    w1d = [A.alloc_bf16(2048) for _ in range(2)]
    yr = [A.alloc_f32(1024) for _ in range(2)]
    t1s = [A.alloc_f32(512) for _ in range(2)]
    z2s = [A.alloc_f32(1024) for _ in range(3)]
    s1r = [A.alloc_f32(1024) for _ in range(2)]
    y2s = [A.alloc_f32(1024) for _ in range(2)]
    outsb = [A.alloc_f32(1024) for _ in range(2)]
    for part in range(4):
        gdma(w2_sb.ap().rearrange("p (c f) -> p c f", c=32)[:, 8 * part:8 * part + 8, :],
             din["w2"].rearrange("(c p) f -> p c f", p=128)[:, 8 * part:8 * part + 8, :],
             waits=list(b2g))
    dW_w2 = dW.n

    w1_mm_after = {}
    g1_pe_after = {}
    z2_dve_after = {}
    fin_v = {}
    fin_g = {}
    fin_a = {}
    for t4 in range(4):
        for fp2 in range(NFF // 2):
            sl = fp2 % 2
            gdma(w1d[sl].ap().rearrange("p (c f) -> p c f", c=8),
                 din["w1"].rearrange("(c p) f -> p c f",
                                     p=128)[:, :, 256 * fp2:256 * fp2 + 256],
                 waits=list(b2g) + [(sp, w1_mm_after.get((t4, fp2 - 2), 0))])
            dW_ff = dW.n
            for fh in range(2):
                ff = 2 * fp2 + fh
                pg, war = rot_pg.acquire()
                terms = [(w1d[sl][:, 256 * c + 128 * fh:256 * c + 128 * fh + 128],
                          y1fm[:, 2048 * c + 512 * t4:2048 * c + 512 * t4 + 512])
                         for c in range(ND)]
                accum(pg, None, terms,
                      [(dW, dW_ff), (sp, g1_pe_after.get(t4 - 1, 0))] + war)
                SC(lambda s, ff=ff, pg=pg:
                   s.activation(g1[:, 512 * ff:512 * ff + 512], pg, AF.Gelu,
                                bias=b1col[:, ff:ff + 1], scale=1.0),
                   waits=[(sp, sp.n), (sp, g1_pe_after.get(t4 - 1, 0))])
                rot_pg.release((sa, sa.n))
            w1_mm_after[(t4, fp2)] = sp.n
        g1_ready = sa.n
        for uu in range(4):
            u = 4 * t4 + uu
            sl = u % 2
            zsl = u % 3
            sdma_in(yr[sl][:, :], ut(yh_d)[u],
                    waits=[(dO, yh_spill_after[u]), (sv, z2_dve_after.get(u - 2, 0))]
                          + (list(b2g) if u < 2 else []))
            sdma_in(s1r[sl][:, :], ut(s1_d)[u],
                    waits=[(dO, dO_s1), (sv, fin_v.get(u - 2, 0)),
                           (sg, fin_g.get(u - 2, 0))])
            for g2 in range(2):
                ph, war_h = rot_ph.acquire()
                terms = [(g1[:, 512 * ff + 128 * uu:512 * ff + 128 * uu + 128],
                          w2_sb[:, 1024 * ff + 512 * g2:1024 * ff + 512 * g2 + 512])
                         for ff in range(NFF)]
                accum(ph, None, terms, [(sa, g1_ready), (dW, dW_w2)] + war_h)
                V(lambda v, sl=sl, g2=g2:
                  v.tensor_tensor(t1s[g2][:, :], yr[sl][:, 512 * g2:512 * g2 + 512],
                                  g1bc[:, 512 * g2:512 * g2 + 512], ALU.mult),
                  waits=[(dA, dA.n), (sg, sg.n)])
                G(lambda g, g2=g2:
                  g.tensor_tensor(t1s[g2][:, :], t1s[g2][:, :],
                                  btbc[:, 512 * g2:512 * g2 + 512], ALU.add),
                  waits=[(sv, sv.n)])
                V(lambda v, zsl=zsl, g2=g2, ph=ph:
                  v.scalar_tensor_tensor(z2s[zsl][:, 512 * g2:512 * g2 + 512],
                                         t1s[g2][:, :], 1.0, ph, ALU.mult, ALU.add),
                  waits=[(sg, sg.n), (sp, sp.n), (sa, fin_a.get(u - 3, 0)),
                         (sv, fin_v.get(u - 3, 0))])
                rot_ph.release((sv, sv.n))
            if dbg:
                sdma_out(ut(z2_d)[u], z2s[zsl][:, :], waits=[(sv, sv.n)])
            z2_dve_after[u] = sv.n
            V(lambda v, zsl=zsl: v.bn_stats(bn6[:, 0:6], z2s[zsl][:, 0:512]))
            V(lambda v, zsl=zsl: v.bn_stats(bn6[:, 6:12], z2s[zsl][:, 512:1024]))
            V(lambda v: v.bn_aggr(mvt, bn6[:, 0:12]))
            SC(lambda s: s.activation(tln, mvt[:, 1:2], AF.Ln, bias=1e-6, scale=1.0),
               waits=[(sv, sv.n)])
            SC(lambda s: s.activation(rstd, tln, AF.Exp, bias=0.0, scale=-0.5))
            V(lambda v: v.tensor_scalar(nmr, mvt[:, 0:1], rstd[:, 0:1], -1.0,
                                        ALU.mult, ALU.mult), waits=[(sa, sa.n)])
            SC(lambda s, sl=sl, zsl=zsl:
               s.activation(y2s[sl][:, :], z2s[zsl][:, :], AF.Identity,
                            bias=nmr[:, 0:1], scale=rstd[:, 0:1]),
               waits=[(sv, sv.n), (sv, fin_v.get(u - 2, 0)),
                      (sg, fin_g.get(u - 2, 0))])
            y2_done = sa.n
            fin_a[u] = sa.n
            V(lambda v, sl=sl: v.tensor_tensor(outsb[sl][:, 0:512],
                                               y2s[sl][:, 0:512],
                                               g2bc[:, 0:512], ALU.mult),
              waits=[(sa, y2_done), (dOUT, 16 * max(0, u - 1))])
            V(lambda v, sl=sl: v.tensor_tensor(outsb[sl][:, 0:512],
                                               outsb[sl][:, 0:512],
                                               s1r[sl][:, 0:512], ALU.add),
              waits=[(dA, dA.n)])
            G(lambda g, sl=sl: g.tensor_tensor(outsb[sl][:, 512:1024],
                                               y2s[sl][:, 512:1024],
                                               g2bc[:, 512:1024], ALU.mult),
              waits=[(sa, y2_done), (dOUT, 16 * max(0, u - 1))])
            G(lambda g, sl=sl: g.tensor_tensor(outsb[sl][:, 512:1024],
                                               outsb[sl][:, 512:1024],
                                               s1r[sl][:, 512:1024], ALU.add),
              waits=[(dA, dA.n)])
            fin_v[u] = sv.n
            fin_g[u] = sg.n
            sdma_out(ut(out_d)[u], outsb[sl][:, :],
                     waits=[(sv, fin_v[u]), (sg, fin_g[u])], final=True)
        g1_pe_after[t4] = sp.n

    return emit_all()


def prep_inputs(inputs):
    f = np.float32
    com = {
        "wq": np.ascontiguousarray(np.asarray(inputs["L0_attn_wq"], f).reshape(D, D)),
        "wk": np.ascontiguousarray(np.asarray(inputs["L0_attn_wk"], f).reshape(D, D)),
        "wv": np.ascontiguousarray(np.asarray(inputs["L0_attn_wv"], f).reshape(D, D)),
        "wo": np.ascontiguousarray(np.asarray(inputs["L0_attn_wo"], f)),
        "omT": np.ascontiguousarray(np.asarray(inputs["L0_attn_omega"], f).T),
        "ln1g": np.ascontiguousarray(np.asarray(inputs["L0_attn_ln1_g"], f)),
        "ln1b": np.ascontiguousarray(np.asarray(inputs["L0_attn_ln1_b"], f)),
        "w1": np.ascontiguousarray(np.asarray(inputs["L0_attn_w1"], f)),
        "b1": np.ascontiguousarray(np.asarray(inputs["L0_attn_b1"], f)),
        "w2": np.ascontiguousarray(np.asarray(inputs["L0_attn_w2"], f)),
        "b2": np.ascontiguousarray(np.asarray(inputs["L0_attn_b2"], f)),
        "ln2g": np.ascontiguousarray(np.asarray(inputs["L0_attn_ln2_g"], f)),
        "ln2b": np.ascontiguousarray(np.asarray(inputs["L0_attn_ln2_b"], f)),
        "wrT": np.ascontiguousarray(np.asarray(inputs["L0_ffn_W_r"], f).T),
        "wkT": np.ascontiguousarray(np.asarray(inputs["L0_ffn_W_k"], f).T),
        "wvT": np.ascontiguousarray(np.asarray(inputs["L0_ffn_W_v"], f).T),
        "mur": np.ascontiguousarray(np.asarray(inputs["L0_ffn_mu_r"], f)),
        "muk": np.ascontiguousarray(np.asarray(inputs["L0_ffn_mu_k"], f)),
    }
    x = np.asarray(inputs["L0_attn_x"], f)
    return x, com


def kernel(**inputs):
    x, com = prep_inputs(inputs)
    nc = build_nc(dbg=False)
    in_maps = []
    for b in range(8):
        m = dict(com)
        m["x"] = np.ascontiguousarray(x[b])
        in_maps.append(m)
    res = run_bass_kernel_spmd(nc, in_maps, core_ids=list(range(8)))
    return np.stack([np.asarray(r["out"], np.float32) for r in res.results], 0)


if __name__ == "__main__":
    nc = build_nc()
    print("built ok, instructions:", len(list(nc.all_instructions())))
